# version 02
# baseline (speedup 1.0000x reference)
import numpy as np
import concourse.bass as bass
import concourse.mybir as mybir
from concourse.bass_utils import run_bass_kernel_spmd

F32 = mybir.dt.float32
BF16 = mybir.dt.bfloat16
I32 = mybir.dt.int32
Alu = mybir.AluOpType
Act = mybir.ActivationFunctionType
AX = mybir.AxisListType.X

P = 128
D = 1024
DC = 8
II = 2816
IC = 22
NT = 4096
NTT = 32
NKB = 16
QB = 8
KM = float(1.5 * 2 ** 23)
EPS = 1e-5
BIG = 3000.0
ASSIGN = [[15, 13, 11, 9, 6, 4, 2, 0],
          [14, 12, 10, 8, 7, 5, 3, 1]]
CJ = [16, 14, 12, 10, 8, 6, 4, 2]
SLOT = [0] * NKB
OFF = [0] * NKB
for _par in (0, 1):
    for _j, _g in enumerate(ASSIGN[_par]):
        SLOT[_g] = _par
        OFF[_g] = _j
GCH = [(k * 512, min(512, II - k * 512)) for k in range(6)]
TRND = [(0, 8), (8, 8), (16, 6)]
NGRP = 7
NAG_A = 48

ENGINES = ("sync", "vector", "scalar", "tensor", "gpsimd")
DEBUG_DUMP = False


class Prog:

    def __init__(self, cnt):
        self.ev = {e: [] for e in ENGINES}
        self.cnt = cnt

    def op(self, eng, fn, incs=(), waits=()):
        w = [(s, v) for (s, v) in waits if v > 0]
        self.ev[eng].append((fn, w, list(incs)))
        for s, n in incs:
            self.cnt[s] = self.cnt.get(s, 0) + n
        return {s: self.cnt[s] for s, _ in incs}

    def now(self, s):
        return self.cnt.get(s, 0)

    def raw(self, eng, fn, bumps=(), waits=()):
        w = [(s, v) for (s, v) in waits if v > 0]
        self.ev[eng].append((fn, w, []))
        for s, n in bumps:
            self.cnt[s] = self.cnt.get(s, 0) + n
        return {s: self.cnt[s] for s, _ in bumps}

    def emit(self, block, sems):
        def mk(eng):
            def body(eobj):
                for fn, waits, incs in self.ev[eng]:
                    for s, v in waits:
                        eobj.wait_ge(sems[s], v)
                    instr = fn(eobj)
                    for s, n in incs:
                        assert instr is not None, f"no instr for inc {s}"
                        instr.then_inc(sems[s], n)
            return body
        block.sync(mk("sync"))
        block.vector(mk("vector"))
        block.scalar(mk("scalar"))
        block.tensor(mk("tensor"))
        block.gpsimd(mk("gpsimd"))


def build():
    nc = bass.Bass("TRN2", target_bir_lowering=False, debug=False,
                   detect_race_conditions=False)
    dt = nc.dram_tensor
    x_in = dt("x_in", [NT, D], F32, kind="ExternalInput").ap()
    wqi = dt("wq", [D, D], F32, kind="ExternalInput").ap()
    wki = dt("wk", [D, D], F32, kind="ExternalInput").ap()
    wvi = dt("wv", [D, D], F32, kind="ExternalInput").ap()
    wprep = dt("wprep", [NGRP, P, II], F32, kind="ExternalInput").ap()
    dtab = dt("dtab", [P, 64], F32, kind="ExternalInput").ap()
    y_out = dt("y", [NT, D], F32, kind="ExternalOutput").ap()
    KD = "ExternalOutput" if DEBUG_DUMP else "Internal"
    x1h = dt("x1h", [NT, D], F32, kind=KD).ap()
    x2h = dt("x2h", [NT, D], F32, kind=KD).ap()
    xaownh = dt("xaownh", [QB, D, 512], BF16, kind=KD).ap()
    srcb = dt("srcb", [QB, D, 512], BF16, kind="Internal").ap()
    bbt = dt("bbt", [QB, 2, D, 512], BF16, kind=KD).ap()
    kth = dt("kth", [NKB, D, 512], BF16, kind=KD).ap()
    vth = dt("vth", [NKB, 512, D], BF16, kind=KD).ap()
    srcA = dt("srcA", [6, D, P], BF16, kind="Internal").ap()
    outA = dt("outA", [NAG_A, D, P], BF16, kind="Internal").ap()
    srcBd = dt("srcBd", [1, II, P], BF16, kind="Internal").ap()
    outB = dt("outB", [8, II, P], BF16, kind="Internal").ap()

    from contextlib import ExitStack
    with ExitStack() as es:
        sbuf = lambda n, s, d=F32: es.enter_context(nc.sbuf_tensor(n, s, d))
        psA = es.enter_context(nc.psum_tensor("psA", [P, 4096], F32))
        ident = sbuf("ident", [P, P], BF16)
        dtb = sbuf("dtb", [P, 64], F32)
        ones = sbuf("ones", [P, 1], BF16)
        qcb = sbuf("qcb", [P, II], F32)
        qqn = sbuf("qqn", [P, II], F32)
        qbm = sbuf("qbm", [P, II // 16], F32)
        qsp = sbuf("qsp", [P, II // 16], F32)
        qrs = sbuf("qrs", [P, II // 16], F32)
        LNW = 176

        def ln_views(lnreg):
            R = {}
            for i, n in enumerate(("s1v", "s2v", "mv", "m2v", "vev", "r0v",
                                   "rsd0", "rsda", "rsdb", "rstdv", "nmrv")):
                R[n] = lnreg[:, i, :]
            R["s1c"] = lnreg[:, 0, 0:1]
            R["s2c"] = lnreg[:, 1, 0:1]
            R["rstdv1"] = lnreg[:, 9, 0:1]
            R["nmrv1"] = lnreg[:, 10, 0:1]
            return R

        SEMS = (["sx0", "sx1", "st0", "st1", "ssrc", "swt", "sy0", "sy1",
                 "sseq", "vseq", "tseq", "gseq", "agc", "swg", "swst",
                 "skv0", "skv1", "skt0", "skt1", "svt0", "svt1", "sxa",
                 "sx1l", "sx2", "gwq", "skst", "bar"])
        semh = {n: es.enter_context(nc.semaphore(n)) for n in SEMS}
        CNT = {n: 0 for n in SEMS}

        ps_gate = [psA[:, 0:512], psA[:, 512:1024]]
        ps_up = [psA[:, 1024:1536], psA[:, 1536:2048]]
        ps_down = [psA[:, 2048:2560], psA[:, 2560:3072]]
        ps_tp = psA[:, 3072:3584].bitcast(BF16)
        ps_tp2 = psA[:, 3584:4096].bitcast(BF16)
        ps_k = [psA[:, 0:512], psA[:, 512:1024]]
        ps_v = [[psA[:, 1024:1536], psA[:, 1536:2048]],
                [psA[:, 2048:2560], psA[:, 2560:3072]]]
        ps_s = [psA[:, 0:512], psA[:, 512:1024]]
        ps_rsv = psA[:, 1024:1536]
        ps_rt = psA[:, 1536:1540]
        ps_q = [psA[:, 0:512], psA[:, 512:1024]]
        ps_o = [[psA[:, (2 * qc + dvh) * 512:(2 * qc + dvh + 1) * 512]
                 for dvh in range(2)] for qc in range(4)]

        def q_pipeline(p, xn_buf, src_ap, dst_ap, w, extra_w_v1=()):
            nb = w // 16
            nbs = II // 16
            xnb = xn_buf[:, 0:w]

            def v1(v):
                v.tensor_reduce(qbm[:, :nbs],
                                xn_buf[:, 0:II].rearrange("p (b k) -> p b k", k=16),
                                AX, Alu.max, apply_absolute_value=True)
                v.tensor_scalar(qsp[:, :nbs], qbm[:, :nbs], 1.0 / 6.0, 1e-12,
                                Alu.mult, Alu.max)
                v.reciprocal(qrs[:, :nbs], qsp[:, :nbs])
                v.tensor_tensor(xnb.rearrange("p (b k) -> p b k", k=16),
                                src_ap.rearrange("p (b k) -> p b k", k=16),
                                qrs[:, :nb].unsqueeze(2).broadcast_to([P, nb, 16]),
                                Alu.mult)
                v.tensor_scalar(qqn[:, :w].bitcast(I32), xnb.bitcast(I32),
                                0x7FFFFFFF, 21, Alu.bitwise_and,
                                Alu.logical_shift_right)
                v.tensor_scalar(qqn[:, :w].bitcast(I32), qqn[:, :w].bitcast(I32),
                                1, 508, Alu.add, Alu.max)
                v.tensor_scalar(qqn[:, :w].bitcast(I32), qqn[:, :w].bitcast(I32),
                                1, 22, Alu.logical_shift_right,
                                Alu.logical_shift_left)
                return v.tensor_scalar(qcb[:, :w], xnb, -1.0, 1.0, Alu.max, Alu.min)

            e1 = p.op("vector", v1, incs=[("vseq", 1)], waits=list(extra_w_v1))

            def s1(s):
                s.activation(qcb[:, :w], qcb[:, :w], Act.Copy, bias=KM, scale=2.0)
                return s.activation(qcb[:, :w], qcb[:, :w], Act.Copy,
                                    bias=-KM / 2, scale=0.5)
            e2 = p.op("scalar", s1, incs=[("sseq", 1)], waits=[("vseq", e1["vseq"])])

            def g1(g):
                return g.tensor_tensor(qcb[:, :w], qcb[:, :w], qqn[:, :w], Alu.mult)
            e3 = p.op("gpsimd", g1, incs=[("gseq", 1)],
                      waits=[("sseq", e2["sseq"]), ("vseq", e1["vseq"])])

            def v2(v):
                return v.tensor_tensor(dst_ap.rearrange("p (b k) -> p b k", k=16),
                                       qcb[:, :w].rearrange("p (b k) -> p b k", k=16),
                                       qsp[:, :nb].unsqueeze(2).broadcast_to(
                                           [P, nb, 16]), Alu.mult)
            e4 = p.op("vector", v2, incs=[("vseq", 1)], waits=[("gseq", e3["gseq"])])
            return e4, e3

        def ln_pipeline(p, sq_waits, x_ap, n, R):
            def s_sq(s):
                s.activation(qcb[:, 0:n], x_ap, Act.Copy, accum_out=R["s1c"])
                return s.activation(qcb[:, 0:n], x_ap, Act.Square,
                                    accum_out=R["s2c"])
            e1 = p.op("scalar", s_sq, incs=[("sseq", 1)], waits=list(sq_waits))

            def v_st(v):
                v.tensor_scalar(R["mv"], R["s1v"], 1.0 / n, None, Alu.mult)
                v.tensor_tensor(R["m2v"], R["mv"], R["mv"], Alu.mult)
                v.tensor_scalar(R["vev"], R["s2v"], 1.0 / n, EPS,
                                Alu.mult, Alu.add)
                v.tensor_tensor(R["vev"], R["vev"], R["m2v"], Alu.subtract)
                return v.reciprocal(R["r0v"], R["vev"])
            e2 = p.op("vector", v_st, incs=[("vseq", 1)], waits=[("sseq", e1["sseq"])])

            def s_rt(s):
                return s.sqrt(R["rsd0"], R["r0v"])
            e3 = p.op("scalar", s_rt, incs=[("sseq", 1)], waits=[("vseq", e2["vseq"])])

            def v_nm(v):
                v.tensor_tensor(R["rsda"], R["rsd0"], R["rsd0"], Alu.mult)
                v.tensor_tensor(R["rsdb"], R["vev"], R["rsda"], Alu.mult)
                v.tensor_scalar(R["rsdb"], R["rsdb"], -0.5, 1.5, Alu.mult, Alu.add)
                v.tensor_tensor(R["rstdv"], R["rsd0"], R["rsdb"], Alu.mult)
                return v.scalar_tensor_tensor(R["nmrv"], R["mv"], -1.0, R["rstdv"],
                                              Alu.mult, Alu.mult)
            return p.op("vector", v_nm, incs=[("vseq", 1)],
                        waits=[("sseq", e3["sseq"])])

        def barrier(p):
            v = p.now("bar") + len(ENGINES)
            for e in ENGINES:
                p.op(e, lambda eng: eng.nop(), incs=[("bar", 1)])
            for e in ENGINES:
                p.ev[e].append((lambda eng: None, [("bar", v)], []))

        with ExitStack() as es0:
            sb0 = lambda n, s, d=F32: es0.enter_context(nc.sbuf_tensor(n, s, d))
            wgb = sb0("wgb", [P, II], F32)
            wqg = sb0("wqg", [P, II], BF16)
            wstg = sb0("wstg", [P, IC, P], BF16)
            iof = sb0("iof", [P, P], F32)
            p0 = Prog(CNT)
            p0.op("sync", lambda e: e.dma_start(out=dtb[:], in_=dtab[:]),
                  incs=[("swg", 16)])

            def g_iota(g):
                return g.iota(iof[:], [[1, P]], base=0, channel_multiplier=-1,
                              allow_small_or_imprecise_dtypes=True)
            eio = p0.op("gpsimd", g_iota, incs=[("gseq", 1)])

            def v_id(v):
                v.memset(ones[:], 1.0)
                return v.tensor_scalar(ident[:], iof[:], 0.0, None, Alu.is_equal)
            p0.op("vector", v_id, incs=[("vseq", 1)], waits=[("gseq", eio["gseq"])])

            for i in range(NGRP):
                el = p0.op("sync",
                           lambda e, i=i: e.dma_start(out=wgb[:], in_=wprep[i]),
                           incs=[("swg", 16)], waits=[("vseq", p0.now("vseq"))])
                eq, _ = q_pipeline(p0, wgb, wgb[:, :], wqg[:, :], II,
                                   extra_w_v1=[("swg", el["swg"])])
                last_evac = None
                for (i0, ntp) in TRND:
                    def t_tp(t, i0=i0, ntp=ntp):
                        r = None
                        for k in range(ntp):
                            r = t.transpose(ps_tp[:, k * P:(k + 1) * P],
                                            wqg[:, (i0 + k) * P:(i0 + k + 1) * P],
                                            ident[:])
                        return r
                    wts = [("vseq", eq["vseq"])]
                    if last_evac is not None:
                        wts.append(("vseq", last_evac["vseq"]))
                    ett = p0.op("tensor", t_tp, incs=[("tseq", 1)], waits=wts)

                    def v_ev(v, i0=i0, ntp=ntp):
                        return v.tensor_copy(
                            wstg[:, i0:i0 + ntp, :],
                            ps_tp[:, 0:ntp * P].rearrange("p (c q) -> p c q", q=P))
                    last_evac = p0.op("vector", v_ev, incs=[("vseq", 1)],
                                      waits=[("tseq", ett["tseq"])])
                if i < 6:
                    def st(e, i=i):
                        return e.dma_start(
                            out=srcA[i].rearrange("(c p) q -> p c q", p=P),
                            in_=wstg[:, 0:DC, :])
                else:
                    def st(e):
                        return e.dma_start(
                            out=srcBd[0].rearrange("(c p) q -> p c q", p=P),
                            in_=wstg[:, :, :])
                p0.op("sync", st, incs=[("swst", 16)],
                      waits=[("vseq", last_evac["vseq"])])

            def ag_a(g):
                return g.collective_compute(
                    "AllGather", Alu.bypass, replica_groups=[list(range(8))],
                    ins=[srcA[:].rearrange("g d p -> (g d) p")],
                    outs=[outA[:].rearrange("g d p -> (g d) p")])
            ea = p0.op("gpsimd", ag_a, incs=[("agc", 1)],
                       waits=[("swst", p0.now("swst"))])
            p0.ev["gpsimd"].append((lambda g: None, [("agc", ea["agc"])], []))

            def ag_b(g):
                return g.collective_compute(
                    "AllGather", Alu.bypass, replica_groups=[list(range(8))],
                    ins=[srcBd[:].rearrange("g d p -> (g d) p")],
                    outs=[outB[:].rearrange("g d p -> (g d) p")])
            eb = p0.op("gpsimd", ag_b, incs=[("agc", 1)])
            p0.ev["gpsimd"].append((lambda g: None, [("agc", eb["agc"])], []))
            AGW = eb["agc"]

            with nc.Block() as blk0:
                p0.emit(blk0, semh)

        def ffn_phase(p, src_hbm, dst_hbm, st_sems, bufs, with_ln2, agw):
            (wgt, wut, wdt, xb, xqb, xqtb, hb, hqb, hqtb, lnreg, stg) = bufs
            R = ln_views(lnreg)
            silub = [qcb[:, 0:512], qcb[:, 512:1024]]

            def v_init(v):
                v.memset(hb[:, :], 1.0)
                return v.memset(lnreg[:].rearrange("p a b -> p (a b)"), 1.0)
            p.op("vector", v_init, incs=[("vseq", 1)])

            swth = semh["swt"]

            def wload(e):
                for g in range(IC):
                    e.dma_start(out=wgt[:, :, g * P:(g + 1) * P],
                                in_=outA[g].rearrange("(c p) q -> p c q", p=P)
                                ).then_inc(swth, 16)
                    e.dma_start(out=wut[:, :, g * P:(g + 1) * P],
                                in_=outA[IC + g].rearrange("(c p) q -> p c q", p=P)
                                ).then_inc(swth, 16)
                for g in range(DC):
                    e.dma_start(out=wdt[:, :, g * P:(g + 1) * P],
                                in_=outB[g].rearrange("(c p) q -> p c q", p=P)
                                ).then_inc(swth, 16)
                return None
            ewl = p.raw("sync", wload, bumps=[("swt", 16 * (2 * IC + DC))],
                        waits=[("agc", agw)])

            prev = {}
            for tt in range(NTT):
                par = tt % 2
                sxn = "sx0" if par == 0 else "sx1"
                stn = st_sems[par]
                xtile = xb[:, par, :]
                lw = []
                if tt >= 2:
                    q = prev[tt - 2]
                    lw = [(stn, q["store"]), ("sseq", q["lastread_s"]),
                          ("vseq", q["lastread_v"])]
                ex = p.op("sync",
                          lambda e, tt=tt, xtile=xtile: e.dma_start(
                              out=xtile, in_=src_hbm[tt * P:(tt + 1) * P, :]),
                          incs=[(sxn, 16)], waits=lw)
                sq_w = [(sxn, ex[sxn])]
                if tt >= 1:
                    sq_w += [("vseq", prev[tt - 1]["ln2stats_v"]),
                             ("gseq", prev[tt - 1]["qcb_free_g"])]
                eln = ln_pipeline(p, sq_w, xtile, D, R)

                def s_norm(s, xtile=xtile):
                    return s.activation(hb[:, 0:D], xtile, Act.Identity,
                                        bias=R["nmrv1"], scale=R["rstdv1"])
                en = p.op("scalar", s_norm, incs=[("sseq", 1)],
                          waits=[("vseq", eln["vseq"])])
                eqa, _ = q_pipeline(p, hb, hb[:, 0:D], xqb[:, :], D,
                                    extra_w_v1=[("sseq", en["sseq"])])

                def t_tpq(t):
                    r = None
                    for k in range(DC):
                        r = t.transpose(ps_tp[:, k * P:(k + 1) * P],
                                        xqb[:, k * P:(k + 1) * P], ident[:])
                    return r
                etq = p.op("tensor", t_tpq, incs=[("tseq", 1)],
                           waits=[("vseq", eqa["vseq"])])

                def v_evq(v):
                    return v.tensor_copy(xqtb[:],
                                         ps_tp[:].rearrange("p (c q) -> p c q", q=P))
                eevq = p.op("vector", v_evq, incs=[("vseq", 1)],
                            waits=[("tseq", etq["tseq"])])

                hchunk_v = []
                for k, (lo, w) in enumerate(GCH):
                    wts = [("vseq", eevq["vseq"]), ("swt", ewl["swt"])]
                    if k >= 2:
                        wts.append(("vseq", hchunk_v[k - 2]["vseq"]))
                    elif tt >= 1:
                        wts.append(("vseq", prev[tt - 1]["hch_v"][4 + k]))

                    def t_gu(t, lo=lo, w=w, k=k):
                        r = None
                        for dc in range(DC):
                            t.matmul(ps_gate[k % 2][:, 0:w], xqtb[:, dc, :],
                                     wgt[:, dc, lo:lo + w],
                                     start=(dc == 0), stop=(dc == DC - 1))
                            r = t.matmul(ps_up[k % 2][:, 0:w], xqtb[:, dc, :],
                                         wut[:, dc, lo:lo + w],
                                         start=(dc == 0), stop=(dc == DC - 1))
                        return r
                    etg = p.op("tensor", t_gu, incs=[("tseq", 1)], waits=wts)

                    def s_si(s, w=w, k=k):
                        return s.activation(silub[k % 2][:, 0:w],
                                            ps_gate[k % 2][:, 0:w], Act.Silu)
                    esi = p.op("scalar", s_si, incs=[("sseq", 1)],
                               waits=[("tseq", etg["tseq"])])

                    def v_h(v, lo=lo, w=w, k=k):
                        return v.tensor_tensor(hb[:, lo:lo + w], ps_up[k % 2][:, 0:w],
                                               silub[k % 2][:, 0:w], Alu.mult)
                    ehm = p.op("vector", v_h, incs=[("vseq", 1)],
                               waits=[("sseq", esi["sseq"])])
                    hchunk_v.append(ehm)
                hch_vals = [e["vseq"] for e in hchunk_v]

                eqh, egh = q_pipeline(p, hb, hb[:, :], hqb[:, :], II)

                evac_s = None
                for (i0, ntp) in TRND:
                    wts = [("vseq", eqh["vseq"])]
                    if evac_s is not None:
                        wts.append(("sseq", evac_s["sseq"]))
                    elif tt >= 1:
                        wts.append(("sseq", prev[tt - 1]["hqev_s"]))

                    def t_th(t, i0=i0, ntp=ntp):
                        r = None
                        for k in range(ntp):
                            r = t.transpose(ps_tp2[:, k * P:(k + 1) * P],
                                            hqb[:, (i0 + k) * P:(i0 + k + 1) * P],
                                            ident[:])
                        return r
                    eth = p.op("tensor", t_th, incs=[("tseq", 1)], waits=wts)

                    def s_ev(s, i0=i0, ntp=ntp):
                        return s.copy(hqtb[:, i0:i0 + ntp, :],
                                      ps_tp2[:, 0:ntp * P].rearrange(
                                          "p (c q) -> p c q", q=P))
                    evac_s = p.op("scalar", s_ev, incs=[("sseq", 1)],
                                  waits=[("tseq", eth["tseq"])])

                dwts = [("sseq", evac_s["sseq"])]
                if tt >= 1:
                    dwts.append(("vseq", prev[tt - 1]["x1_v"]))

                def t_dn(t):
                    r = None
                    for ic in range(IC):
                        t.matmul(ps_down[0], hqtb[:, ic, :], wdt[:, ic, 0:512],
                                 start=(ic == 0), stop=(ic == IC - 1))
                        r = t.matmul(ps_down[1], hqtb[:, ic, :], wdt[:, ic, 512:D],
                                     start=(ic == 0), stop=(ic == IC - 1))
                    return r
                edn = p.op("tensor", t_dn, incs=[("tseq", 1)], waits=dwts)

                def v_x1(v, xtile=xtile):
                    v.scalar_tensor_tensor(xtile[:, 0:512], ps_down[0], 0.5,
                                           xtile[:, 0:512], Alu.mult, Alu.add)
                    return v.scalar_tensor_tensor(xtile[:, 512:D], ps_down[1], 0.5,
                                                  xtile[:, 512:D], Alu.mult, Alu.add)
                ex1 = p.op("vector", v_x1, incs=[("vseq", 1)],
                           waits=[("tseq", edn["tseq"])])
                est = p.op("sync",
                           lambda e, tt=tt, xtile=xtile: e.dma_start(
                               out=dst_hbm[tt * P:(tt + 1) * P, :], in_=xtile),
                           incs=[(stn, 16)], waits=[("vseq", ex1["vseq"])])
                rec = {"store": est[stn], "x1_v": ex1["vseq"], "hch_v": hch_vals,
                       "hqev_s": evac_s["sseq"], "qcb_free_g": egh["gseq"]}

                if with_ln2:
                    eln2 = ln_pipeline(p, [("vseq", ex1["vseq"]),
                                           ("gseq", egh["gseq"])], xtile, D, R)

                    def s_xa(s, xtile=xtile):
                        return s.activation(xqb[:], xtile, Act.Identity,
                                            bias=R["nmrv1"], scale=R["rstdv1"])
                    exa = p.op("scalar", s_xa, incs=[("sseq", 1)],
                               waits=[("vseq", eln2["vseq"])])

                    def t_ta(t):
                        r = None
                        for k in range(DC):
                            r = t.transpose(ps_tp[:, k * P:(k + 1) * P],
                                            xqb[:, k * P:(k + 1) * P], ident[:])
                        return r
                    eta = p.op("tensor", t_ta, incs=[("tseq", 1)],
                               waits=[("sseq", exa["sseq"])])
                    rnd, sl = tt // 4, tt % 4

                    def v_ea(v, sl=sl):
                        return v.tensor_copy(
                            stg[:, :, sl * P:(sl + 1) * P],
                            ps_tp[:].rearrange("p (c q) -> p c q", q=P))
                    wts = [("tseq", eta["tseq"])]
                    if sl == 0 and rnd >= 1:
                        wts.append(("ssrc", 32 * rnd))
                    eea = p.op("vector", v_ea, incs=[("vseq", 1)], waits=wts)
                    rec["ln2stats_v"] = eln2["vseq"]
                    if sl == 3:
                        ssrch = semh["ssrc"]

                        def st_src(e, rnd=rnd):
                            e.dma_start(
                                out=srcb[rnd].rearrange("(c p) t -> p c t", p=P),
                                in_=stg[:]).then_inc(ssrch, 16)
                            e.dma_start(
                                out=xaownh[rnd].rearrange("(c p) t -> p c t", p=P),
                                in_=stg[:]).then_inc(ssrch, 16)
                            return None
                        esrc = p.raw("sync", st_src, bumps=[("ssrc", 32)],
                                     waits=[("vseq", eea["vseq"])])

                        def g_ag(g, rnd=rnd):
                            return g.collective_compute(
                                "AllGather", Alu.bypass,
                                replica_groups=[[0, 1], [2, 3], [4, 5], [6, 7]],
                                ins=[srcb[rnd]],
                                outs=[bbt[rnd].rearrange("s d t -> (s d) t")])
                        eag = p.op("gpsimd", g_ag, incs=[("agc", 1)],
                                   waits=[("ssrc", esrc["ssrc"])])
                        p.ev["gpsimd"].append(
                            (lambda g: None, [("agc", eag["agc"])], []))
                else:
                    rec["ln2stats_v"] = ex1["vseq"]
                rec["lastread_s"] = p.now("sseq")
                rec["lastread_v"] = p.now("vseq")
                prev[tt] = rec
            p.ev["sync"].append((lambda e: None,
                                 [(st_sems[0], p.now(st_sems[0])),
                                  (st_sems[1], p.now(st_sems[1]))], []))

        with ExitStack() as es1:
            sb1 = lambda n, s, d=F32: es1.enter_context(nc.sbuf_tensor(n, s, d))
            bufs1 = (sb1("wgt", [P, DC, II], BF16), sb1("wut", [P, DC, II], BF16),
                     sb1("wdt", [P, IC, D], BF16), sb1("xb", [P, 2, D], F32),
                     sb1("xqb", [P, D], BF16), sb1("xqtb", [P, DC, P], BF16),
                     sb1("hb", [P, II], F32), sb1("hqb", [P, II], BF16),
                     sb1("hqtb", [P, IC, P], BF16), sb1("lnreg1", [P, 11, 176], F32),
                     sb1("stg", [P, DC, 512], BF16))
            p1 = Prog(CNT)
            ffn_phase(p1, x_in, x1h, ("st0", "st1"), bufs1, True, AGW)
            AGXA = p1.now("agc")
            barrier(p1)
            with nc.Block() as blk1:
                p1.emit(blk1, semh)

        with ExitStack() as es3:
            sb3 = lambda n, s, d=F32: es3.enter_context(nc.sbuf_tensor(n, s, d))
            wq16 = sb3("wq16", [P, DC, D], BF16)
            wk16 = sb3("wk16", [P, DC, D], BF16)
            wv16 = sb3("wv16", [P, DC, D], BF16)
            qtb = sb3("qtb", [P, DC, 512], BF16)
            xaq = sb3("xaq", [P, DC, 512], BF16)
            ktb = sb3("ktb", [P, 2, DC, 512], BF16)
            vb = sb3("vb", [P, 2, 4, D], BF16)
            ptb = sb3("ptb", [P, 64, 512], BF16)
            x1j = sb3("x1j", [P, 4, D], F32)
            d3 = sb3("d3", [P, 512], F32)
            mskb = sb3("mskb", [P, 512], F32)
            rsTin = mskb[:, 0:LNW]
            rsTw = mskb[:, 256:256 + LNW]
            rse = sb3("rse", [1, 512], F32)
            onesf = sb3("onesf", [1, 1], F32)
            p3 = Prog(CNT)

            def g_d3(g):
                return g.iota(d3[:], [[int(BIG), 512]], base=0,
                              channel_multiplier=-int(BIG),
                              allow_small_or_imprecise_dtypes=True)
            p3.op("gpsimd", g_d3, incs=[("gseq", 1)])
            p3.op("vector", lambda v: v.memset(onesf[:], 1.0), incs=[("vseq", 1)])

            gwqh = semh["gwq"]

            def g_cast(g):
                g.dma_start(out=wq16[:],
                            in_=wqi[:].rearrange("(c p) d -> p c d", p=P)
                            ).then_inc(gwqh, 16)
                g.dma_start(out=wk16[:],
                            in_=wki[:].rearrange("(c p) d -> p c d", p=P)
                            ).then_inc(gwqh, 16)
                g.dma_start(out=wv16[:],
                            in_=wvi[:].rearrange("(c p) d -> p c d", p=P)
                            ).then_inc(gwqh, 16)
                return None
            egc = p3.raw("gpsimd", g_cast, bumps=[("gwq", 48)])

            kst = {}
            for e2 in range(NKB):
                kg = NKB - 1 - e2
                par = e2 % 2
                skvn = "skv0" if par == 0 else "skv1"
                lwts = [("agc", AGXA - QB + OFF[kg] + 1)]
                if e2 >= 2:
                    lwts.append(("tseq", kst[e2 - 2]["t_end"]))

                def l_kv(e, kg=kg, par=par):
                    return e.dma_start(
                        out=ktb[:, par, :, :],
                        in_=bbt[OFF[kg], SLOT[kg]].rearrange("(c p) t -> p c t", p=P))
                ekl = p3.op("sync", l_kv, incs=[(skvn, 16)], waits=lwts)

                ks_evts = []
                for dkc in range(DC):
                    wts = [(skvn, ekl[skvn]), ("gwq", egc["gwq"])]
                    if dkc >= 2:
                        wts.append(("sseq", ks_evts[dkc - 2]["sseq"]))
                    elif e2 >= 1:
                        wts.append(("sseq", kst[e2 - 1]["kev_s"]))

                    def t_k(t, dkc=dkc, par=par):
                        r = None
                        for dc in range(DC):
                            r = t.matmul(ps_k[dkc % 2],
                                         wk16[:, dc, dkc * P:(dkc + 1) * P],
                                         ktb[:, par, dc, :],
                                         start=(dc == 0), stop=(dc == DC - 1))
                        return r
                    etk = p3.op("tensor", t_k, incs=[("tseq", 1)], waits=wts)
                    sw = [("tseq", etk["tseq"])]
                    if e2 >= 1 and dkc == 0:
                        sw.append(("skst", kst[e2 - 1]["kst"]))

                    def s_ke(s, dkc=dkc):
                        return s.copy(ptb[:, dkc, :], ps_k[dkc % 2])
                    eke = p3.op("scalar", s_ke, incs=[("sseq", 1)], waits=sw)
                    ks_evts.append(eke)
                ekst = p3.op("sync",
                             lambda e, kg=kg: e.dma_start(
                                 out=kth[kg].rearrange("(c p) t -> p c t", p=P),
                                 in_=ptb[:, 0:DC, :]),
                             incs=[("skst", 16)],
                             waits=[("sseq", ks_evts[-1]["sseq"])])

                vs_evts = []
                for tc in range(4):
                    wts = []
                    if tc >= 2:
                        wts.append(("vseq", vs_evts[tc - 2]["vseq"]))
                    elif e2 >= 1:
                        wts.append(("vseq", kst[e2 - 1]["vev_v"]))

                    def t_v(t, tc=tc, par=par):
                        r = None
                        for dc in range(DC):
                            t.matmul(ps_v[tc % 2][0],
                                     ktb[:, par, dc, tc * P:(tc + 1) * P],
                                     wv16[:, dc, 0:512],
                                     start=(dc == 0), stop=(dc == DC - 1))
                            r = t.matmul(ps_v[tc % 2][1],
                                         ktb[:, par, dc, tc * P:(tc + 1) * P],
                                         wv16[:, dc, 512:D],
                                         start=(dc == 0), stop=(dc == DC - 1))
                        return r
                    etv = p3.op("tensor", t_v, incs=[("tseq", 1)], waits=wts)
                    vw = [("tseq", etv["tseq"])]
                    if e2 >= 1 and tc == 0:
                        vw.append(("skst", kst[e2 - 1]["vst"]))

                    def v_ve(v, tc=tc):
                        v.tensor_copy(ptb[:, 8 + 2 * tc, :], ps_v[tc % 2][0])
                        return v.tensor_copy(ptb[:, 9 + 2 * tc, :], ps_v[tc % 2][1])
                    eve = p3.op("vector", v_ve, incs=[("vseq", 1)], waits=vw)
                    vs_evts.append(eve)
                evst = p3.op("sync",
                             lambda e, kg=kg: e.dma_start(
                                 out=vth[kg].rearrange("(tc p) d -> p tc d", p=P),
                                 in_=ptb[:, 8:16, :].rearrange(
                                     "p (tc h) t -> p tc (h t)", h=2)),
                             incs=[("skst", 16)],
                             waits=[("vseq", vs_evts[-1]["vseq"])])
                kst[e2] = {"t_end": p3.now("tseq"), "kev_s": ks_evts[-1]["sseq"],
                           "vev_v": vs_evts[-1]["vseq"], "kst": ekst["skst"],
                           "vst": evst["skst"]}
            P3A_SKST = p3.now("skst")
            P3A_TSEQ = p3.now("tseq")

            prevj = None
            for j in range(QB):
                cj = CJ[j]
                lw = [("skst", P3A_SKST)]
                if j >= 1:
                    lw.append(("tseq", prevj["q_end"]))
                exq = p3.op("sync",
                            lambda e, j=j: e.dma_start(
                                out=xaq[:],
                                in_=xaownh[j].rearrange("(c p) t -> p c t", p=P)),
                            incs=[("sxa", 16)], waits=lw)
                lw2 = [("sx2", prevj["x2st"])] if j >= 1 else []
                ex1l = p3.op("sync",
                             lambda e, j=j: e.dma_start(
                                 out=x1j[:],
                                 in_=x1h[j * 512:(j + 1) * 512, :].rearrange(
                                     "(q p) d -> p q d", p=P)),
                             incs=[("sx1l", 16)], waits=lw2)

                q_evts = []
                for dkc in range(DC):
                    wts = [("sxa", exq["sxa"]), ("gwq", egc["gwq"])]
                    if dkc >= 2:
                        wts.append(("sseq", q_evts[dkc - 2]["sseq"]))
                    if dkc == 0 and j >= 1:
                        wts.append(("vseq", prevj["x2_v"]))

                    def t_q(t, dkc=dkc):
                        r = None
                        for dc in range(DC):
                            r = t.matmul(ps_q[dkc % 2],
                                         wq16[:, dc, dkc * P:(dkc + 1) * P],
                                         xaq[:, dc, :],
                                         start=(dc == 0), stop=(dc == DC - 1))
                        return r
                    etq2 = p3.op("tensor", t_q, incs=[("tseq", 1)], waits=wts)

                    def s_qe(s, dkc=dkc):
                        return s.copy(qtb[:, dkc, :], ps_q[dkc % 2])
                    eqe = p3.op("scalar", s_qe, incs=[("sseq", 1)],
                                waits=[("tseq", etq2["tseq"])])
                    q_evts.append(eqe)
                q_end = p3.now("tseq")

                smm = {}
                expv = {}
                ktload = {}
                for kg in range(cj):
                    sktn = "skt0" if kg % 2 == 0 else "skt1"
                    lkw = [("tseq", P3A_TSEQ if j == 0 and kg < 2 else 0)]
                    if kg >= 2:
                        lkw.append(("tseq", smm[(kg - 2, 3)]))
                    elif j >= 1 and prevj["nkt"] >= 2 - kg:
                        lkw.append(("tseq", prevj["smm_last"][kg - 2]))
                    ekt = p3.op("sync",
                                lambda e, kg=kg: e.dma_start(
                                    out=ktb[:, kg % 2, :, :],
                                    in_=kth[kg].rearrange("(c p) t -> p c t", p=P)),
                                incs=[(sktn, 16)], waits=lkw)
                    ktload[kg] = ekt[sktn]
                    for kc in range(4):
                        slot = kg * 4 + kc
                        wts = [(sktn, ektl) for ektl in [ktload[kg]]]
                        wts.append(("sseq", q_evts[-1]["sseq"]))
                        if slot >= 2:
                            wts.append(("sseq", expv[slot - 2]))
                        elif j >= 1:
                            wts.append(("sseq", prevj["exp_last"][slot - 2]))
                        if slot == 0:
                            wts.append(("vseq", prevj["rsT_v"] if j >= 1 else 0))

                        def t_s(t, kg=kg, kc=kc):
                            r = None
                            for dc in range(DC):
                                r = t.matmul(ps_s[kc % 2],
                                             ktb[:, kg % 2, dc, kc * P:(kc + 1) * P],
                                             qtb[:, dc, :],
                                             start=(dc == 0), stop=(dc == DC - 1))
                            return r
                        ets = p3.op("tensor", t_s, incs=[("tseq", 1)], waits=wts)
                        smm[(kg, kc)] = ets["tseq"]

                        if kg >= cj - 2:
                            tb = kg - (cj - 2)

                            def v_m(v, kc=kc, tb=tb, j=j):
                                ix = (2 * j + tb) * 4 + kc
                                v.tensor_scalar(mskb[:], d3[:],
                                                dtb[:, ix:ix + 1],
                                                0.0, Alu.add, Alu.min)
                                return v.tensor_tensor(ps_s[kc % 2], ps_s[kc % 2],
                                                       mskb[:], Alu.add)
                            evm = p3.op("vector", v_m, incs=[("vseq", 1)],
                                        waits=[("tseq", ets["tseq"])])
                            ewait = [("vseq", evm["vseq"])]
                        else:
                            ewait = [("tseq", ets["tseq"])]

                        def s_e(s, slot=slot, kc=kc):
                            return s.activation(ptb[:, slot, :], ps_s[kc % 2],
                                                Act.Exp, bias=0.0, scale=1.0 / 32.0)
                        ese = p3.op("scalar", s_e, incs=[("sseq", 1)], waits=ewait)
                        expv[slot] = ese["sseq"]

                        def t_rs(t, slot=slot, cj=cj):
                            return t.matmul(ps_rsv[0:1, 0:512], ones[:],
                                            ptb[:, slot, :], start=(slot == 0),
                                            stop=(slot == cj * 4 - 1))
                        p3.op("tensor", t_rs, incs=[("tseq", 1)],
                              waits=[("sseq", ese["sseq"])])

                def v_rs(v):
                    return v.tensor_scalar(rse[:], ps_rsv[0:1, 0:512], 1e-30, None,
                                           Alu.add)
                ers = p3.op("vector", v_rs, incs=[("vseq", 1)],
                            waits=[("tseq", p3.now("tseq"))])

                def t_rt(t):
                    r = None
                    for qc in range(4):
                        r = t.matmul(ps_rt[:, qc:qc + 1],
                                     rse[0:1, qc * P:(qc + 1) * P], onesf[:],
                                     start=True, stop=True)
                    return r
                ert = p3.op("tensor", t_rt, incs=[("tseq", 1)],
                            waits=[("vseq", ers["vseq"])])

                ems = p3.op("vector", lambda v: v.memset(rsTin, 1.0),
                            incs=[("vseq", 1)])

                def s_rt2(s):
                    return s.copy(rsTin[:, 0:4], ps_rt[:])
                ecrt = p3.op("scalar", s_rt2, incs=[("sseq", 1)],
                             waits=[("tseq", ert["tseq"]), ("vseq", ems["vseq"])])

                def v_rt(v):
                    return v.reciprocal(rsTw, rsTin)
                ertv = p3.op("vector", v_rt, incs=[("vseq", 1)],
                             waits=[("sseq", ecrt["sseq"])])

                pv_t = {}
                for kg in range(cj):
                    svtn = "svt0" if kg % 2 == 0 else "svt1"
                    vw = []
                    if kg >= 2:
                        vw.append(("tseq", pv_t[kg - 2]))
                    elif j >= 1 and kg - 2 + prevj["cj"] >= 0:
                        vw.append(("tseq", prevj["pv_last"][kg - 2]))
                    evl = p3.op("sync",
                                lambda e, kg=kg: e.dma_start(
                                    out=vb[:, kg % 2, :, :],
                                    in_=vth[kg].rearrange("(tc p) d -> p tc d", p=P)),
                                incs=[(svtn, 16)], waits=vw)

                    wts = [(svtn, evl[svtn]), ("sseq", expv[cj * 4 - 1]),
                           ("vseq", ertv["vseq"])]

                    def t_pv(t, kg=kg, cj=cj):
                        r = None
                        for tc in range(4):
                            for qc in range(4):
                                for dvh in range(2):
                                    r = t.matmul(
                                        ps_o[qc][dvh],
                                        ptb[:, kg * 4 + tc, qc * P:(qc + 1) * P],
                                        vb[:, kg % 2, tc, dvh * 512:(dvh + 1) * 512],
                                        start=(kg == 0 and tc == 0),
                                        stop=(kg == cj - 1 and tc == 3))
                        return r
                    etpv = p3.op("tensor", t_pv, incs=[("tseq", 1)], waits=wts)
                    pv_t[kg] = etpv["tseq"]

                def v_x2(v):
                    r = None
                    for qc in range(4):
                        for dvh in range(2):
                            r = v.scalar_tensor_tensor(
                                x1j[:, qc, dvh * 512:(dvh + 1) * 512],
                                ps_o[qc][dvh], rsTw[:, qc:qc + 1],
                                x1j[:, qc, dvh * 512:(dvh + 1) * 512],
                                Alu.mult, Alu.add)
                    return r
                ex2 = p3.op("vector", v_x2, incs=[("vseq", 1)],
                            waits=[("tseq", pv_t[cj - 1]), ("sx1l", ex1l["sx1l"])])
                ex2s = p3.op("sync",
                             lambda e, j=j: e.dma_start(
                                 out=x2h[j * 512:(j + 1) * 512, :].rearrange(
                                     "(q p) d -> p q d", p=P),
                                 in_=x1j[:]),
                             incs=[("sx2", 16)], waits=[("vseq", ex2["vseq"])])
                prevj = {"q_end": q_end, "x2st": ex2s["sx2"], "x2_v": ex2["vseq"],
                         "qev_s": q_evts[-1]["sseq"], "nkt": cj, "cj": cj,
                         "smm_last": {-2: smm[(cj - 2, 3)], -1: smm[(cj - 1, 3)]},
                         "exp_last": {-2: expv[cj * 4 - 2], -1: expv[cj * 4 - 1]},
                         "pv_last": {-2: pv_t[cj - 2], -1: pv_t[cj - 1]},
                         "rsT_v": ertv["vseq"]}
            p3.ev["sync"].append((lambda e: None,
                                  [("sx2", p3.now("sx2"))], []))
            barrier(p3)
            with nc.Block() as blk3:
                p3.emit(blk3, semh)

        with ExitStack() as es4:
            sb4 = lambda n, s, d=F32: es4.enter_context(nc.sbuf_tensor(n, s, d))
            bufs4 = (sb4("wgt4", [P, DC, II], BF16), sb4("wut4", [P, DC, II], BF16),
                     sb4("wdt4", [P, IC, D], BF16), sb4("xb4", [P, 2, D], F32),
                     sb4("xqb4", [P, D], BF16), sb4("xqtb4", [P, DC, P], BF16),
                     sb4("hb4", [P, II], F32), sb4("hqb4", [P, II], BF16),
                     sb4("hqtb4", [P, IC, P], BF16), sb4("lnreg4", [P, 11, 176], F32),
                     sb4("stg4", [P, DC, 512], BF16))
            p4 = Prog(CNT)
            ffn_phase(p4, x2h, y_out, ("sy0", "sy1"), bufs4, False, AGW)
            p4.ev["sync"].append((lambda e: None,
                                  [("sy0", p4.now("sy0")), ("sy1", p4.now("sy1"))],
                                  []))
            with nc.Block() as blk4:
                p4.emit(blk4, semh)
    return nc


_NC_CACHE = None


def kernel(L0_attn_x, L0_attn_Wq, L0_attn_Wk, L0_attn_Wv,
           L0_ffn_w_gate, L0_ffn_w_up, L0_ffn_w_down):
    global _NC_CACHE
    x = np.ascontiguousarray(np.asarray(L0_attn_x, dtype=np.float32))
    wq = np.ascontiguousarray(np.asarray(L0_attn_Wq, dtype=np.float32))
    wk = np.ascontiguousarray(np.asarray(L0_attn_Wk, dtype=np.float32))
    wv = np.ascontiguousarray(np.asarray(L0_attn_Wv, dtype=np.float32))
    wg = np.ascontiguousarray(np.asarray(L0_ffn_w_gate, dtype=np.float32))
    wu = np.ascontiguousarray(np.asarray(L0_ffn_w_up, dtype=np.float32))
    wd = np.ascontiguousarray(np.asarray(L0_ffn_w_down, dtype=np.float32))
    B, T, _ = x.shape

    acat = np.zeros((NAG_A * P, D), np.float32)
    acat[0:II] = wg
    acat[IC * P:IC * P + II] = wu

    in_maps = []
    for c in range(8):
        b, par = c // 2, c % 2
        xb_seq = x[b].reshape(NKB, 512, D)
        x_own = np.concatenate([xb_seq[g] for g in ASSIGN[par]], axis=0)
        dvals = np.zeros(64, np.float32)
        for j in range(QB):
            for tb in range(2):
                kg = CJ[j] - 2 + tb
                for kc in range(4):
                    dvals[(2 * j + tb) * 4 + kc] = BIG * (
                        512.0 * (ASSIGN[par][j] - kg) - 128.0 * kc)
        dtabv = np.broadcast_to(dvals[None, :], (P, 64)).copy()
        wp = np.zeros((NGRP, P, II), np.float32)
        for i in range(6):
            wp[i, :, 0:D] = acat[(6 * c + i) * P:(6 * c + i + 1) * P]
        wp[6] = wd[c * P:(c + 1) * P]
        in_maps.append({"x_in": np.ascontiguousarray(x_own),
                        "wq": wq, "wk": wk, "wv": wv,
                        "wprep": np.ascontiguousarray(wp),
                        "dtab": np.ascontiguousarray(dtabv)})
    if _NC_CACHE is None:
        _NC_CACHE = build()
    res = run_bass_kernel_spmd(_NC_CACHE, in_maps, core_ids=list(range(8)))
    out = np.empty_like(x)
    for c in range(8):
        b, par = c // 2, c % 2
        yv = np.asarray(res.results[c]["y"]).reshape(QB, 512, D)
        for j, g in enumerate(ASSIGN[par]):
            out[b, g * 512:(g + 1) * 512, :] = yv[j]
    return out


# version 03
# speedup vs baseline: 1.0852x; 1.0852x over previous
import numpy as np
import concourse.bass as bass
import concourse.mybir as mybir
from concourse.bass_utils import run_bass_kernel_spmd

F32 = mybir.dt.float32
BF16 = mybir.dt.bfloat16
I32 = mybir.dt.int32
Alu = mybir.AluOpType
Act = mybir.ActivationFunctionType
AX = mybir.AxisListType.X

P = 128
D = 1024
DC = 8
II = 2816
IC = 22
NT = 4096
NTT = 32
NKB = 16
QB = 8
KM = float(1.5 * 2 ** 23)
EPS = 1e-5
BIG = 3000.0
ASSIGN = [[15, 13, 11, 9, 6, 4, 2, 0],
          [14, 12, 10, 8, 7, 5, 3, 1]]
CJ = [16, 14, 12, 10, 8, 6, 4, 2]
SLOT = [0] * NKB
OFF = [0] * NKB
for _par in (0, 1):
    for _j, _g in enumerate(ASSIGN[_par]):
        SLOT[_g] = _par
        OFF[_g] = _j
GCH = [(k * 512, min(512, II - k * 512)) for k in range(6)]
TRND = [(0, 8), (8, 8), (16, 6)]
NGRP = 7
NAG_A = 48

ENGINES = ("sync", "vector", "scalar", "tensor", "gpsimd")
DEBUG_DUMP = False
PROGS = []


class Prog:

    def __init__(self, cnt):
        self.ev = {e: [] for e in ENGINES}
        self.cnt = cnt
        PROGS.append(self)

    def op(self, eng, fn, incs=(), waits=()):
        w = [(s, v) for (s, v) in waits if v > 0]
        self.ev[eng].append((fn, w, list(incs), []))
        for s, n in incs:
            self.cnt[s] = self.cnt.get(s, 0) + n
        return {s: self.cnt[s] for s, _ in incs}

    def now(self, s):
        return self.cnt.get(s, 0)

    def raw(self, eng, fn, bumps=(), waits=()):
        w = [(s, v) for (s, v) in waits if v > 0]
        self.ev[eng].append((fn, w, [], list(bumps)))
        for s, n in bumps:
            self.cnt[s] = self.cnt.get(s, 0) + n
        return {s: self.cnt[s] for s, _ in bumps}

    def emit(self, block, sems):
        def mk(eng):
            def body(eobj):
                for fn, waits, incs, _bumps in self.ev[eng]:
                    for s, v in waits:
                        eobj.wait_ge(sems[s], v)
                    instr = fn(eobj)
                    for s, n in incs:
                        assert instr is not None, f"no instr for inc {s}"
                        instr.then_inc(sems[s], n)
            return body
        block.sync(mk("sync"))
        block.vector(mk("vector"))
        block.scalar(mk("scalar"))
        block.tensor(mk("tensor"))
        block.gpsimd(mk("gpsimd"))


def build():
    nc = bass.Bass("TRN2", target_bir_lowering=False, debug=False,
                   detect_race_conditions=False)
    dt = nc.dram_tensor
    x_in = dt("x_in", [NT, D], F32, kind="ExternalInput").ap()
    wqi = dt("wq", [D, D], F32, kind="ExternalInput").ap()
    wki = dt("wk", [D, D], F32, kind="ExternalInput").ap()
    wvi = dt("wv", [D, D], F32, kind="ExternalInput").ap()
    wprep = dt("wprep", [NGRP, P, II], F32, kind="ExternalInput").ap()
    dtab = dt("dtab", [P, 64], F32, kind="ExternalInput").ap()
    y_out = dt("y", [NT, D], F32, kind="ExternalOutput").ap()
    KD = "ExternalOutput" if DEBUG_DUMP else "Internal"
    x1h = dt("x1h", [NT, D], F32, kind=KD).ap()
    x2h = dt("x2h", [NT, D], F32, kind=KD).ap()
    xaownh = dt("xaownh", [QB, D, 512], BF16, kind=KD).ap()
    srcb = dt("srcb", [QB, D, 512], BF16, kind="Internal").ap()
    bbt = dt("bbt", [QB, 2, D, 512], BF16, kind=KD).ap()
    kth = dt("kth", [NKB, D, 512], BF16, kind=KD).ap()
    vth = dt("vth", [NKB, 512, D], BF16, kind=KD).ap()
    srcA = dt("srcA", [6, D, P], BF16, kind="Internal").ap()
    outA = dt("outA", [NAG_A, D, P], BF16, kind="Internal").ap()
    srcBd = dt("srcBd", [1, II, P], BF16, kind="Internal").ap()
    outB = dt("outB", [8, II, P], BF16, kind="Internal").ap()

    from contextlib import ExitStack
    with ExitStack() as es:
        sbuf = lambda n, s, d=F32: es.enter_context(nc.sbuf_tensor(n, s, d))
        psA = es.enter_context(nc.psum_tensor("psA", [P, 4096], F32))
        ident = sbuf("ident", [P, P], BF16)
        dtb = sbuf("dtb", [P, 64], F32)
        ones = sbuf("ones", [P, 1], BF16)
        qcb = sbuf("qcb", [P, II], F32)
        qqn = sbuf("qqn", [P, II], F32)
        qbm = sbuf("qbm", [P, II // 16], F32)
        qsp = sbuf("qsp", [P, II // 16], F32)
        qrs = sbuf("qrs", [P, II // 16], F32)
        LNW = 176

        def ln_views(lnreg):
            R = {}
            for i, n in enumerate(("s1v", "s2v", "mv", "m2v", "vev", "r0v",
                                   "rsd0", "rsda", "rsdb", "rstdv", "nmrv")):
                R[n] = lnreg[:, i, :]
            R["s1c"] = lnreg[:, 0, 0:1]
            R["s2c"] = lnreg[:, 1, 0:1]
            R["rstdv1"] = lnreg[:, 9, 0:1]
            R["nmrv1"] = lnreg[:, 10, 0:1]
            return R

        SEMS = (["sx0", "sx1", "st0", "st1", "ssrc", "swt", "sy0", "sy1",
                 "sseq", "vseq", "tseq", "gseq", "agc", "swg", "swst",
                 "skv0", "skv1", "skt0", "skt1", "svt0", "svt1", "sxa",
                 "sx1l", "sx2", "gwq", "skst", "bar"])
        semh = {n: es.enter_context(nc.semaphore(n)) for n in SEMS}
        CNT = {n: 0 for n in SEMS}

        ps_gate = [psA[:, 0:512], psA[:, 512:1024]]
        ps_up = [psA[:, 1024:1536], psA[:, 1536:2048]]
        ps_down = [psA[:, 2048:2560], psA[:, 2560:3072]]
        ps_tp = psA[:, 3072:3584].bitcast(BF16)
        ps_tp2 = psA[:, 3584:4096].bitcast(BF16)
        ps_k = [psA[:, 0:512], psA[:, 512:1024]]
        ps_v = [[psA[:, 1024:1536], psA[:, 1536:2048]],
                [psA[:, 2048:2560], psA[:, 2560:3072]]]
        ps_s = [psA[:, 0:512], psA[:, 512:1024]]
        ps_rsv = psA[:, 1024:1536]
        ps_rt = psA[:, 1536:1540]
        ps_q = [psA[:, 0:512], psA[:, 512:1024]]
        ps_o = [[psA[:, (2 * qc + dvh) * 512:(2 * qc + dvh + 1) * 512]
                 for dvh in range(2)] for qc in range(4)]

        def q_pipeline(p, xn_buf, src_ap, dst_ap, w, extra_w_v1=()):
            nb = w // 16
            nbs = II // 16
            xnb = xn_buf[:, 0:w]

            def v1(v):
                v.tensor_reduce(qbm[:, :nbs],
                                xn_buf[:, 0:II].rearrange("p (b k) -> p b k", k=16),
                                AX, Alu.max, apply_absolute_value=True)
                v.tensor_scalar(qsp[:, :nbs], qbm[:, :nbs], 1.0 / 6.0, 1e-12,
                                Alu.mult, Alu.max)
                v.reciprocal(qrs[:, :nbs], qsp[:, :nbs])
                v.tensor_tensor(xnb.rearrange("p (b k) -> p b k", k=16),
                                src_ap.rearrange("p (b k) -> p b k", k=16),
                                qrs[:, :nb].unsqueeze(2).broadcast_to([P, nb, 16]),
                                Alu.mult)
                v.tensor_scalar(qqn[:, :w].bitcast(I32), xnb.bitcast(I32),
                                0x7FFFFFFF, 21, Alu.bitwise_and,
                                Alu.logical_shift_right)
                v.tensor_scalar(qqn[:, :w].bitcast(I32), qqn[:, :w].bitcast(I32),
                                1, 508, Alu.add, Alu.max)
                v.tensor_scalar(qqn[:, :w].bitcast(I32), qqn[:, :w].bitcast(I32),
                                1, 22, Alu.logical_shift_right,
                                Alu.logical_shift_left)
                v.tensor_scalar(qcb[:, :w], xnb, -1.0, 1.0, Alu.max, Alu.min)
                v.tensor_scalar(qcb[:, :w], qcb[:, :w], 2.0, KM,
                                Alu.mult, Alu.add)
                v.tensor_scalar(qcb[:, :w], qcb[:, :w], 0.5, -KM / 2,
                                Alu.mult, Alu.add)
                v.tensor_tensor(qcb[:, :w], qcb[:, :w], qqn[:, :w], Alu.mult)
                return v.tensor_tensor(dst_ap.rearrange("p (b k) -> p b k", k=16),
                                       qcb[:, :w].rearrange("p (b k) -> p b k", k=16),
                                       qsp[:, :nb].unsqueeze(2).broadcast_to(
                                           [P, nb, 16]), Alu.mult)
            e4 = p.op("vector", v1, incs=[("vseq", 1)], waits=list(extra_w_v1))
            return e4, e4

        def ln_pipeline(p, sq_waits, x_ap, n, R):
            def s_sq(s):
                s.activation(qcb[:, 0:n], x_ap, Act.Copy, accum_out=R["s1c"])
                return s.activation(qcb[:, 0:n], x_ap, Act.Square,
                                    accum_out=R["s2c"])
            e1 = p.op("scalar", s_sq, incs=[("sseq", 1)], waits=list(sq_waits))

            def v_st(v):
                v.tensor_scalar(R["mv"], R["s1v"], 1.0 / n, None, Alu.mult)
                v.tensor_tensor(R["m2v"], R["mv"], R["mv"], Alu.mult)
                v.tensor_scalar(R["vev"], R["s2v"], 1.0 / n, EPS,
                                Alu.mult, Alu.add)
                v.tensor_tensor(R["vev"], R["vev"], R["m2v"], Alu.subtract)
                v.tensor_scalar(R["r0v"], R["vev"], 0.5, None, Alu.mult)
                v.tensor_scalar(R["rsd0"].bitcast(I32), R["vev"].bitcast(I32),
                                1, None, Alu.logical_shift_right)
                v.tensor_scalar(R["rsd0"].bitcast(I32), R["rsd0"].bitcast(I32),
                                0xFFFFFFFF, None, Alu.bitwise_xor)
                v.tensor_scalar(R["rsd0"].bitcast(I32), R["rsd0"].bitcast(I32),
                                0x5f3759df + 1, None, Alu.add)
                for it in range(3):
                    last = it == 2
                    v.tensor_tensor(R["rsda"], R["rsd0"], R["rsd0"], Alu.mult)
                    v.tensor_tensor(R["rsda"], R["r0v"], R["rsda"], Alu.mult)
                    v.tensor_scalar(R["rsda"], R["rsda"], -1.0, 1.5,
                                    Alu.mult, Alu.add)
                    v.tensor_tensor(R["rstdv"] if last else R["rsd0"],
                                    R["rsd0"], R["rsda"], Alu.mult)
                return v.scalar_tensor_tensor(R["nmrv"], R["mv"], -1.0, R["rstdv"],
                                              Alu.mult, Alu.mult)
            return p.op("vector", v_st, incs=[("vseq", 1)],
                        waits=[("sseq", e1["sseq"])])

        def barrier(p):
            v = p.now("bar") + len(ENGINES)
            for e in ENGINES:
                p.op(e, lambda eng: eng.nop(), incs=[("bar", 1)])
            for e in ENGINES:
                p.ev[e].append((lambda eng: None, [("bar", v)], [], []))

        with ExitStack() as es0:
            sb0 = lambda n, s, d=F32: es0.enter_context(nc.sbuf_tensor(n, s, d))
            wgb = sb0("wgb", [P, II], F32)
            wqg = sb0("wqg", [P, II], BF16)
            wstg = sb0("wstg", [P, IC, P], BF16)
            iof = sb0("iof", [P, P], F32)
            p0 = Prog(CNT)
            p0.op("sync", lambda e: e.dma_start(out=dtb[:], in_=dtab[:]),
                  incs=[("swg", 16)])

            def g_iota(g):
                return g.iota(iof[:], [[1, P]], base=0, channel_multiplier=-1,
                              allow_small_or_imprecise_dtypes=True)
            eio = p0.op("gpsimd", g_iota, incs=[("gseq", 1)])

            def v_id(v):
                v.memset(ones[:], 1.0)
                return v.tensor_scalar(ident[:], iof[:], 0.0, None, Alu.is_equal)
            p0.op("vector", v_id, incs=[("vseq", 1)], waits=[("gseq", eio["gseq"])])

            for i in range(NGRP):
                el = p0.op("sync",
                           lambda e, i=i: e.dma_start(out=wgb[:], in_=wprep[i]),
                           incs=[("swg", 16)], waits=[("vseq", p0.now("vseq"))])
                eq, _ = q_pipeline(p0, wgb, wgb[:, :], wqg[:, :], II,
                                   extra_w_v1=[("swg", el["swg"])])
                last_evac = None
                for (i0, ntp) in TRND:
                    def t_tp(t, i0=i0, ntp=ntp):
                        r = None
                        for k in range(ntp):
                            r = t.transpose(ps_tp[:, k * P:(k + 1) * P],
                                            wqg[:, (i0 + k) * P:(i0 + k + 1) * P],
                                            ident[:])
                        return r
                    wts = [("vseq", eq["vseq"])]
                    if last_evac is not None:
                        wts.append(("vseq", last_evac["vseq"]))
                    ett = p0.op("tensor", t_tp, incs=[("tseq", 1)], waits=wts)

                    def v_ev(v, i0=i0, ntp=ntp):
                        return v.tensor_copy(
                            wstg[:, i0:i0 + ntp, :],
                            ps_tp[:, 0:ntp * P].rearrange("p (c q) -> p c q", q=P))
                    last_evac = p0.op("vector", v_ev, incs=[("vseq", 1)],
                                      waits=[("tseq", ett["tseq"])])
                if i < 6:
                    def st(e, i=i):
                        return e.dma_start(
                            out=srcA[i].rearrange("(c p) q -> p c q", p=P),
                            in_=wstg[:, 0:DC, :])
                else:
                    def st(e):
                        return e.dma_start(
                            out=srcBd[0].rearrange("(c p) q -> p c q", p=P),
                            in_=wstg[:, :, :])
                p0.op("sync", st, incs=[("swst", 16)],
                      waits=[("vseq", last_evac["vseq"])])

            def ag_a(g):
                return g.collective_compute(
                    "AllGather", Alu.bypass, replica_groups=[list(range(8))],
                    ins=[srcA[:].rearrange("g d p -> (g d) p")],
                    outs=[outA[:].rearrange("g d p -> (g d) p")])
            ea = p0.op("gpsimd", ag_a, incs=[("agc", 1)],
                       waits=[("swst", p0.now("swst"))])
            p0.ev["gpsimd"].append((lambda g: None, [("agc", ea["agc"])], [], []))

            def ag_b(g):
                return g.collective_compute(
                    "AllGather", Alu.bypass, replica_groups=[list(range(8))],
                    ins=[srcBd[:].rearrange("g d p -> (g d) p")],
                    outs=[outB[:].rearrange("g d p -> (g d) p")])
            eb = p0.op("gpsimd", ag_b, incs=[("agc", 1)])
            p0.ev["gpsimd"].append((lambda g: None, [("agc", eb["agc"])], [], []))
            AGW = eb["agc"]

            with nc.Block() as blk0:
                p0.emit(blk0, semh)

        def ffn_phase(p, src_hbm, dst_hbm, st_sems, bufs, with_ln2, agw):
            (wgt, wut, wdt, xb, xqb, xqtb, hb, hqb, hqtb, lnreg, stg) = bufs
            R = ln_views(lnreg)
            silub = [qcb[:, 0:512], qcb[:, 512:1024]]

            def v_init(v):
                v.memset(hb[:, :], 1.0)
                return v.memset(lnreg[:].rearrange("p a b -> p (a b)"), 1.0)
            p.op("vector", v_init, incs=[("vseq", 1)])

            swth = semh["swt"]

            def wload(e):
                for g in range(IC):
                    e.dma_start(out=wgt[:, :, g * P:(g + 1) * P],
                                in_=outA[g].rearrange("(c p) q -> p c q", p=P)
                                ).then_inc(swth, 16)
                    e.dma_start(out=wut[:, :, g * P:(g + 1) * P],
                                in_=outA[IC + g].rearrange("(c p) q -> p c q", p=P)
                                ).then_inc(swth, 16)
                for g in range(DC):
                    e.dma_start(out=wdt[:, :, g * P:(g + 1) * P],
                                in_=outB[g].rearrange("(c p) q -> p c q", p=P)
                                ).then_inc(swth, 16)
                return None
            ewl = p.raw("sync", wload, bumps=[("swt", 16 * (2 * IC + DC))],
                        waits=[("agc", agw)])

            prev = {}
            for tt in range(NTT):
                par = tt % 2
                sxn = "sx0" if par == 0 else "sx1"
                stn = st_sems[par]
                xtile = xb[:, par, :]
                lw = []
                if tt >= 2:
                    q = prev[tt - 2]
                    lw = [(stn, q["store"]), ("sseq", q["lastread_s"]),
                          ("vseq", q["lastread_v"])]
                ex = p.op("sync",
                          lambda e, tt=tt, xtile=xtile: e.dma_start(
                              out=xtile, in_=src_hbm[tt * P:(tt + 1) * P, :]),
                          incs=[(sxn, 16)], waits=lw)
                sq_w = [(sxn, ex[sxn])]
                if tt >= 1:
                    sq_w += [("vseq", prev[tt - 1]["ln2stats_v"]),
                             ("vseq", prev[tt - 1]["qcb_free_g"])]
                eln = ln_pipeline(p, sq_w, xtile, D, R)

                def s_norm(s, xtile=xtile):
                    return s.activation(hb[:, 0:D], xtile, Act.Identity,
                                        bias=R["nmrv1"], scale=R["rstdv1"])
                en = p.op("scalar", s_norm, incs=[("sseq", 1)],
                          waits=[("vseq", eln["vseq"])])
                eqa, _ = q_pipeline(p, hb, hb[:, 0:D], xqb[:, :], D,
                                    extra_w_v1=[("sseq", en["sseq"])])

                def t_tpq(t):
                    r = None
                    for k in range(DC):
                        r = t.transpose(ps_tp[:, k * P:(k + 1) * P],
                                        xqb[:, k * P:(k + 1) * P], ident[:])
                    return r
                etq = p.op("tensor", t_tpq, incs=[("tseq", 1)],
                           waits=[("vseq", eqa["vseq"])])

                def v_evq(v):
                    return v.tensor_copy(xqtb[:],
                                         ps_tp[:].rearrange("p (c q) -> p c q", q=P))
                eevq = p.op("vector", v_evq, incs=[("vseq", 1)],
                            waits=[("tseq", etq["tseq"])])

                hchunk_v = []
                for k, (lo, w) in enumerate(GCH):
                    wts = [("vseq", eevq["vseq"]), ("swt", ewl["swt"])]
                    if k >= 2:
                        wts.append(("vseq", hchunk_v[k - 2]["vseq"]))
                    elif tt >= 1:
                        wts.append(("vseq", prev[tt - 1]["hch_v"][4 + k]))

                    def t_gu(t, lo=lo, w=w, k=k):
                        r = None
                        for dc in range(DC):
                            t.matmul(ps_gate[k % 2][:, 0:w], xqtb[:, dc, :],
                                     wgt[:, dc, lo:lo + w],
                                     start=(dc == 0), stop=(dc == DC - 1))
                            r = t.matmul(ps_up[k % 2][:, 0:w], xqtb[:, dc, :],
                                         wut[:, dc, lo:lo + w],
                                         start=(dc == 0), stop=(dc == DC - 1))
                        return r
                    etg = p.op("tensor", t_gu, incs=[("tseq", 1)], waits=wts)

                    def s_si(s, w=w, k=k):
                        return s.activation(silub[k % 2][:, 0:w],
                                            ps_gate[k % 2][:, 0:w], Act.Silu)
                    esi = p.op("scalar", s_si, incs=[("sseq", 1)],
                               waits=[("tseq", etg["tseq"])])

                    def v_h(v, lo=lo, w=w, k=k):
                        return v.tensor_tensor(hb[:, lo:lo + w], ps_up[k % 2][:, 0:w],
                                               silub[k % 2][:, 0:w], Alu.mult)
                    ehm = p.op("vector", v_h, incs=[("vseq", 1)],
                               waits=[("sseq", esi["sseq"])])
                    hchunk_v.append(ehm)
                hch_vals = [e["vseq"] for e in hchunk_v]

                eqh, egh = q_pipeline(p, hb, hb[:, :], hqb[:, :], II)

                evac_s = None
                for (i0, ntp) in TRND:
                    wts = [("vseq", eqh["vseq"])]
                    if evac_s is not None:
                        wts.append(("sseq", evac_s["sseq"]))
                    elif tt >= 1:
                        wts.append(("sseq", prev[tt - 1]["hqev_s"]))

                    def t_th(t, i0=i0, ntp=ntp):
                        r = None
                        for k in range(ntp):
                            r = t.transpose(ps_tp2[:, k * P:(k + 1) * P],
                                            hqb[:, (i0 + k) * P:(i0 + k + 1) * P],
                                            ident[:])
                        return r
                    eth = p.op("tensor", t_th, incs=[("tseq", 1)], waits=wts)

                    def s_ev(s, i0=i0, ntp=ntp):
                        return s.copy(hqtb[:, i0:i0 + ntp, :],
                                      ps_tp2[:, 0:ntp * P].rearrange(
                                          "p (c q) -> p c q", q=P))
                    evac_s = p.op("scalar", s_ev, incs=[("sseq", 1)],
                                  waits=[("tseq", eth["tseq"])])

                dwts = [("sseq", evac_s["sseq"])]
                if tt >= 1:
                    dwts.append(("vseq", prev[tt - 1]["x1_v"]))

                def t_dn(t):
                    r = None
                    for ic in range(IC):
                        t.matmul(ps_down[0], hqtb[:, ic, :], wdt[:, ic, 0:512],
                                 start=(ic == 0), stop=(ic == IC - 1))
                        r = t.matmul(ps_down[1], hqtb[:, ic, :], wdt[:, ic, 512:D],
                                     start=(ic == 0), stop=(ic == IC - 1))
                    return r
                edn = p.op("tensor", t_dn, incs=[("tseq", 1)], waits=dwts)

                def v_x1(v, xtile=xtile):
                    v.scalar_tensor_tensor(xtile[:, 0:512], ps_down[0], 0.5,
                                           xtile[:, 0:512], Alu.mult, Alu.add)
                    return v.scalar_tensor_tensor(xtile[:, 512:D], ps_down[1], 0.5,
                                                  xtile[:, 512:D], Alu.mult, Alu.add)
                ex1 = p.op("vector", v_x1, incs=[("vseq", 1)],
                           waits=[("tseq", edn["tseq"])])
                est = p.op("sync",
                           lambda e, tt=tt, xtile=xtile: e.dma_start(
                               out=dst_hbm[tt * P:(tt + 1) * P, :], in_=xtile),
                           incs=[(stn, 16)], waits=[("vseq", ex1["vseq"])])
                rec = {"store": est[stn], "x1_v": ex1["vseq"], "hch_v": hch_vals,
                       "hqev_s": evac_s["sseq"], "qcb_free_g": egh["vseq"]}

                if with_ln2:
                    eln2 = ln_pipeline(p, [("vseq", ex1["vseq"]),
                                           ("vseq", egh["vseq"])], xtile, D, R)

                    def s_xa(s, xtile=xtile):
                        return s.activation(xqb[:], xtile, Act.Identity,
                                            bias=R["nmrv1"], scale=R["rstdv1"])
                    exa = p.op("scalar", s_xa, incs=[("sseq", 1)],
                               waits=[("vseq", eln2["vseq"])])

                    def t_ta(t):
                        r = None
                        for k in range(DC):
                            r = t.transpose(ps_tp[:, k * P:(k + 1) * P],
                                            xqb[:, k * P:(k + 1) * P], ident[:])
                        return r
                    eta = p.op("tensor", t_ta, incs=[("tseq", 1)],
                               waits=[("sseq", exa["sseq"])])
                    rnd, sl = tt // 4, tt % 4

                    def v_ea(v, sl=sl):
                        return v.tensor_copy(
                            stg[:, :, sl * P:(sl + 1) * P],
                            ps_tp[:].rearrange("p (c q) -> p c q", q=P))
                    wts = [("tseq", eta["tseq"])]
                    if sl == 0 and rnd >= 1:
                        wts.append(("ssrc", 32 * rnd))
                    eea = p.op("vector", v_ea, incs=[("vseq", 1)], waits=wts)
                    rec["ln2stats_v"] = eln2["vseq"]
                    if sl == 3:
                        ssrch = semh["ssrc"]

                        def st_src(e, rnd=rnd):
                            e.dma_start(
                                out=srcb[rnd].rearrange("(c p) t -> p c t", p=P),
                                in_=stg[:]).then_inc(ssrch, 16)
                            e.dma_start(
                                out=xaownh[rnd].rearrange("(c p) t -> p c t", p=P),
                                in_=stg[:]).then_inc(ssrch, 16)
                            return None
                        esrc = p.raw("sync", st_src, bumps=[("ssrc", 32)],
                                     waits=[("vseq", eea["vseq"])])

                        def g_ag(g, rnd=rnd):
                            return g.collective_compute(
                                "AllGather", Alu.bypass,
                                replica_groups=[[0, 1], [2, 3], [4, 5], [6, 7]],
                                ins=[srcb[rnd]],
                                outs=[bbt[rnd].rearrange("s d t -> (s d) t")])
                        eag = p.op("gpsimd", g_ag, incs=[("agc", 1)],
                                   waits=[("ssrc", esrc["ssrc"])])
                        p.ev["gpsimd"].append(
                            (lambda g: None, [("agc", eag["agc"])], [], []))
                else:
                    rec["ln2stats_v"] = ex1["vseq"]
                rec["lastread_s"] = p.now("sseq")
                rec["lastread_v"] = p.now("vseq")
                prev[tt] = rec
            p.ev["sync"].append((lambda e: None,
                                 [(st_sems[0], p.now(st_sems[0])),
                                  (st_sems[1], p.now(st_sems[1]))], [], []))

        with ExitStack() as es1:
            sb1 = lambda n, s, d=F32: es1.enter_context(nc.sbuf_tensor(n, s, d))
            bufs1 = (sb1("wgt", [P, DC, II], BF16), sb1("wut", [P, DC, II], BF16),
                     sb1("wdt", [P, IC, D], BF16), sb1("xb", [P, 2, D], F32),
                     sb1("xqb", [P, D], BF16), sb1("xqtb", [P, DC, P], BF16),
                     sb1("hb", [P, II], F32), sb1("hqb", [P, II], BF16),
                     sb1("hqtb", [P, IC, P], BF16), sb1("lnreg1", [P, 11, 176], F32),
                     sb1("stg", [P, DC, 512], BF16))
            p1 = Prog(CNT)
            ffn_phase(p1, x_in, x1h, ("st0", "st1"), bufs1, True, AGW)
            AGXA = p1.now("agc")
            barrier(p1)
            with nc.Block() as blk1:
                p1.emit(blk1, semh)

        with ExitStack() as es3:
            sb3 = lambda n, s, d=F32: es3.enter_context(nc.sbuf_tensor(n, s, d))
            wq16 = sb3("wq16", [P, DC, D], BF16)
            wk16 = sb3("wk16", [P, DC, D], BF16)
            wv16 = sb3("wv16", [P, DC, D], BF16)
            qtb = sb3("qtb", [P, DC, 512], BF16)
            xaq = sb3("xaq", [P, DC, 512], BF16)
            ktb = sb3("ktb", [P, 2, DC, 512], BF16)
            vb = sb3("vb", [P, 2, 4, D], BF16)
            ptb = sb3("ptb", [P, 64, 512], BF16)
            x1j = sb3("x1j", [P, 4, D], F32)
            d3 = sb3("d3", [P, 512], F32)
            mskb = sb3("mskb", [P, 512], F32)
            rsTin = mskb[:, 0:LNW]
            rsTw = mskb[:, 256:256 + LNW]
            rse = sb3("rse", [1, 512], F32)
            onesf = sb3("onesf", [1, 1], F32)
            p3 = Prog(CNT)

            def g_d3(g):
                return g.iota(d3[:], [[int(BIG), 512]], base=0,
                              channel_multiplier=-int(BIG),
                              allow_small_or_imprecise_dtypes=True)
            p3.op("gpsimd", g_d3, incs=[("gseq", 1)])
            p3.op("vector", lambda v: v.memset(onesf[:], 1.0), incs=[("vseq", 1)])

            gwqh = semh["gwq"]

            def g_cast(g):
                g.dma_start(out=wq16[:],
                            in_=wqi[:].rearrange("(c p) d -> p c d", p=P)
                            ).then_inc(gwqh, 16)
                g.dma_start(out=wk16[:],
                            in_=wki[:].rearrange("(c p) d -> p c d", p=P)
                            ).then_inc(gwqh, 16)
                g.dma_start(out=wv16[:],
                            in_=wvi[:].rearrange("(c p) d -> p c d", p=P)
                            ).then_inc(gwqh, 16)
                return None
            egc = p3.raw("gpsimd", g_cast, bumps=[("gwq", 48)])

            kst = {}
            for e2 in range(NKB):
                kg = NKB - 1 - e2
                par = e2 % 2
                skvn = "skv0" if par == 0 else "skv1"
                lwts = [("agc", AGXA - QB + OFF[kg] + 1)]
                if e2 >= 2:
                    lwts.append(("tseq", kst[e2 - 2]["t_end"]))

                def l_kv(e, kg=kg, par=par):
                    return e.dma_start(
                        out=ktb[:, par, :, :],
                        in_=bbt[OFF[kg], SLOT[kg]].rearrange("(c p) t -> p c t", p=P))
                ekl = p3.op("sync", l_kv, incs=[(skvn, 16)], waits=lwts)

                ks_evts = []
                for dkc in range(DC):
                    wts = [(skvn, ekl[skvn]), ("gwq", egc["gwq"])]
                    if dkc >= 2:
                        wts.append(("sseq", ks_evts[dkc - 2]["sseq"]))
                    elif e2 >= 1:
                        wts.append(("sseq", kst[e2 - 1]["kev_s"]))

                    def t_k(t, dkc=dkc, par=par):
                        r = None
                        for dc in range(DC):
                            r = t.matmul(ps_k[dkc % 2],
                                         wk16[:, dc, dkc * P:(dkc + 1) * P],
                                         ktb[:, par, dc, :],
                                         start=(dc == 0), stop=(dc == DC - 1))
                        return r
                    etk = p3.op("tensor", t_k, incs=[("tseq", 1)], waits=wts)
                    sw = [("tseq", etk["tseq"])]
                    if e2 >= 1 and dkc == 0:
                        sw.append(("skst", kst[e2 - 1]["kst"]))

                    def s_ke(s, dkc=dkc):
                        return s.copy(ptb[:, dkc, :], ps_k[dkc % 2])
                    eke = p3.op("scalar", s_ke, incs=[("sseq", 1)], waits=sw)
                    ks_evts.append(eke)
                ekst = p3.op("sync",
                             lambda e, kg=kg: e.dma_start(
                                 out=kth[kg].rearrange("(c p) t -> p c t", p=P),
                                 in_=ptb[:, 0:DC, :]),
                             incs=[("skst", 16)],
                             waits=[("sseq", ks_evts[-1]["sseq"])])

                vs_evts = []
                for tc in range(4):
                    wts = []
                    if tc >= 2:
                        wts.append(("vseq", vs_evts[tc - 2]["vseq"]))
                    elif e2 >= 1:
                        wts.append(("vseq", kst[e2 - 1]["vev_v"]))

                    def t_v(t, tc=tc, par=par):
                        r = None
                        for dc in range(DC):
                            t.matmul(ps_v[tc % 2][0],
                                     ktb[:, par, dc, tc * P:(tc + 1) * P],
                                     wv16[:, dc, 0:512],
                                     start=(dc == 0), stop=(dc == DC - 1))
                            r = t.matmul(ps_v[tc % 2][1],
                                         ktb[:, par, dc, tc * P:(tc + 1) * P],
                                         wv16[:, dc, 512:D],
                                         start=(dc == 0), stop=(dc == DC - 1))
                        return r
                    etv = p3.op("tensor", t_v, incs=[("tseq", 1)], waits=wts)
                    vw = [("tseq", etv["tseq"])]
                    if e2 >= 1 and tc == 0:
                        vw.append(("skst", kst[e2 - 1]["vst"]))

                    def v_ve(v, tc=tc):
                        v.tensor_copy(ptb[:, 8 + 2 * tc, :], ps_v[tc % 2][0])
                        return v.tensor_copy(ptb[:, 9 + 2 * tc, :], ps_v[tc % 2][1])
                    eve = p3.op("vector", v_ve, incs=[("vseq", 1)], waits=vw)
                    vs_evts.append(eve)
                evst = p3.op("sync",
                             lambda e, kg=kg: e.dma_start(
                                 out=vth[kg].rearrange("(tc p) d -> p tc d", p=P),
                                 in_=ptb[:, 8:16, :].rearrange(
                                     "p (tc h) t -> p tc (h t)", h=2)),
                             incs=[("skst", 16)],
                             waits=[("vseq", vs_evts[-1]["vseq"])])
                kst[e2] = {"t_end": p3.now("tseq"), "kev_s": ks_evts[-1]["sseq"],
                           "vev_v": vs_evts[-1]["vseq"], "kst": ekst["skst"],
                           "vst": evst["skst"]}
            P3A_SKST = p3.now("skst")
            P3A_TSEQ = p3.now("tseq")

            prevj = None
            for j in range(QB):
                cj = CJ[j]
                lw = [("skst", P3A_SKST)]
                if j >= 1:
                    lw.append(("tseq", prevj["q_end"]))
                exq = p3.op("sync",
                            lambda e, j=j: e.dma_start(
                                out=xaq[:],
                                in_=xaownh[j].rearrange("(c p) t -> p c t", p=P)),
                            incs=[("sxa", 16)], waits=lw)
                lw2 = [("sx2", prevj["x2st"])] if j >= 1 else []
                ex1l = p3.op("sync",
                             lambda e, j=j: e.dma_start(
                                 out=x1j[:],
                                 in_=x1h[j * 512:(j + 1) * 512, :].rearrange(
                                     "(q p) d -> p q d", p=P)),
                             incs=[("sx1l", 16)], waits=lw2)

                q_evts = []
                for dkc in range(DC):
                    wts = [("sxa", exq["sxa"]), ("gwq", egc["gwq"])]
                    if dkc >= 2:
                        wts.append(("sseq", q_evts[dkc - 2]["sseq"]))
                    if dkc == 0 and j >= 1:
                        wts.append(("vseq", prevj["x2_v"]))

                    def t_q(t, dkc=dkc):
                        r = None
                        for dc in range(DC):
                            r = t.matmul(ps_q[dkc % 2],
                                         wq16[:, dc, dkc * P:(dkc + 1) * P],
                                         xaq[:, dc, :],
                                         start=(dc == 0), stop=(dc == DC - 1))
                        return r
                    etq2 = p3.op("tensor", t_q, incs=[("tseq", 1)], waits=wts)

                    def s_qe(s, dkc=dkc):
                        return s.copy(qtb[:, dkc, :], ps_q[dkc % 2])
                    eqe = p3.op("scalar", s_qe, incs=[("sseq", 1)],
                                waits=[("tseq", etq2["tseq"])])
                    q_evts.append(eqe)
                q_end = p3.now("tseq")

                smm = {}
                expv = {}
                ktload = {}
                for kg in range(cj):
                    sktn = "skt0" if kg % 2 == 0 else "skt1"
                    lkw = [("tseq", P3A_TSEQ if j == 0 and kg < 2 else 0)]
                    if kg >= 2:
                        lkw.append(("tseq", smm[(kg - 2, 3)]))
                    elif j >= 1 and prevj["nkt"] >= 2 - kg:
                        lkw.append(("tseq", prevj["smm_last"][kg - 2]))
                    ekt = p3.op("sync",
                                lambda e, kg=kg: e.dma_start(
                                    out=ktb[:, kg % 2, :, :],
                                    in_=kth[kg].rearrange("(c p) t -> p c t", p=P)),
                                incs=[(sktn, 16)], waits=lkw)
                    ktload[kg] = ekt[sktn]
                    for kc in range(4):
                        slot = kg * 4 + kc
                        wts = [(sktn, ektl) for ektl in [ktload[kg]]]
                        wts.append(("sseq", q_evts[-1]["sseq"]))
                        if slot >= 2:
                            wts.append(("sseq", expv[slot - 2]))
                        elif j >= 1:
                            wts.append(("sseq", prevj["exp_last"][slot - 2]))
                        if slot == 0:
                            wts.append(("vseq", prevj["rsT_v"] if j >= 1 else 0))

                        def t_s(t, kg=kg, kc=kc):
                            r = None
                            for dc in range(DC):
                                r = t.matmul(ps_s[kc % 2],
                                             ktb[:, kg % 2, dc, kc * P:(kc + 1) * P],
                                             qtb[:, dc, :],
                                             start=(dc == 0), stop=(dc == DC - 1))
                            return r
                        ets = p3.op("tensor", t_s, incs=[("tseq", 1)], waits=wts)
                        smm[(kg, kc)] = ets["tseq"]

                        if kg >= cj - 2:
                            tb = kg - (cj - 2)

                            def v_m(v, kc=kc, tb=tb, j=j):
                                ix = (2 * j + tb) * 4 + kc
                                v.tensor_scalar(mskb[:], d3[:],
                                                dtb[:, ix:ix + 1],
                                                0.0, Alu.add, Alu.min)
                                return v.tensor_tensor(ps_s[kc % 2], ps_s[kc % 2],
                                                       mskb[:], Alu.add)
                            evm = p3.op("vector", v_m, incs=[("vseq", 1)],
                                        waits=[("tseq", ets["tseq"])])
                            ewait = [("vseq", evm["vseq"])]
                        else:
                            ewait = [("tseq", ets["tseq"])]

                        def s_e(s, slot=slot, kc=kc):
                            return s.activation(ptb[:, slot, :], ps_s[kc % 2],
                                                Act.Exp, bias=0.0, scale=1.0 / 32.0)
                        ese = p3.op("scalar", s_e, incs=[("sseq", 1)], waits=ewait)
                        expv[slot] = ese["sseq"]

                        def t_rs(t, slot=slot, cj=cj):
                            return t.matmul(ps_rsv[0:1, 0:512], ones[:],
                                            ptb[:, slot, :], start=(slot == 0),
                                            stop=(slot == cj * 4 - 1))
                        p3.op("tensor", t_rs, incs=[("tseq", 1)],
                              waits=[("sseq", ese["sseq"])])

                def v_rs(v):
                    return v.tensor_scalar(rse[:], ps_rsv[0:1, 0:512], 1e-30, None,
                                           Alu.add)
                ers = p3.op("vector", v_rs, incs=[("vseq", 1)],
                            waits=[("tseq", p3.now("tseq"))])

                def t_rt(t):
                    r = None
                    for qc in range(4):
                        r = t.matmul(ps_rt[:, qc:qc + 1],
                                     rse[0:1, qc * P:(qc + 1) * P], onesf[:],
                                     start=True, stop=True)
                    return r
                ert = p3.op("tensor", t_rt, incs=[("tseq", 1)],
                            waits=[("vseq", ers["vseq"])])

                ems = p3.op("vector", lambda v: v.memset(rsTin, 1.0),
                            incs=[("vseq", 1)])

                def s_rt2(s):
                    return s.copy(rsTin[:, 0:4], ps_rt[:])
                ecrt = p3.op("scalar", s_rt2, incs=[("sseq", 1)],
                             waits=[("tseq", ert["tseq"]), ("vseq", ems["vseq"])])

                def v_rt(v):
                    return v.reciprocal(rsTw, rsTin)
                ertv = p3.op("vector", v_rt, incs=[("vseq", 1)],
                             waits=[("sseq", ecrt["sseq"])])

                pv_t = {}
                for kg in range(cj):
                    svtn = "svt0" if kg % 2 == 0 else "svt1"
                    vw = []
                    if kg >= 2:
                        vw.append(("tseq", pv_t[kg - 2]))
                    elif j >= 1 and kg - 2 + prevj["cj"] >= 0:
                        vw.append(("tseq", prevj["pv_last"][kg - 2]))
                    evl = p3.op("sync",
                                lambda e, kg=kg: e.dma_start(
                                    out=vb[:, kg % 2, :, :],
                                    in_=vth[kg].rearrange("(tc p) d -> p tc d", p=P)),
                                incs=[(svtn, 16)], waits=vw)

                    wts = [(svtn, evl[svtn]), ("sseq", expv[cj * 4 - 1]),
                           ("vseq", ertv["vseq"])]

                    def t_pv(t, kg=kg, cj=cj):
                        r = None
                        for tc in range(4):
                            for qc in range(4):
                                for dvh in range(2):
                                    r = t.matmul(
                                        ps_o[qc][dvh],
                                        ptb[:, kg * 4 + tc, qc * P:(qc + 1) * P],
                                        vb[:, kg % 2, tc, dvh * 512:(dvh + 1) * 512],
                                        start=(kg == 0 and tc == 0),
                                        stop=(kg == cj - 1 and tc == 3))
                        return r
                    etpv = p3.op("tensor", t_pv, incs=[("tseq", 1)], waits=wts)
                    pv_t[kg] = etpv["tseq"]

                def v_x2(v):
                    r = None
                    for qc in range(4):
                        for dvh in range(2):
                            r = v.scalar_tensor_tensor(
                                x1j[:, qc, dvh * 512:(dvh + 1) * 512],
                                ps_o[qc][dvh], rsTw[:, qc:qc + 1],
                                x1j[:, qc, dvh * 512:(dvh + 1) * 512],
                                Alu.mult, Alu.add)
                    return r
                ex2 = p3.op("vector", v_x2, incs=[("vseq", 1)],
                            waits=[("tseq", pv_t[cj - 1]), ("sx1l", ex1l["sx1l"])])
                ex2s = p3.op("sync",
                             lambda e, j=j: e.dma_start(
                                 out=x2h[j * 512:(j + 1) * 512, :].rearrange(
                                     "(q p) d -> p q d", p=P),
                                 in_=x1j[:]),
                             incs=[("sx2", 16)], waits=[("vseq", ex2["vseq"])])
                prevj = {"q_end": q_end, "x2st": ex2s["sx2"], "x2_v": ex2["vseq"],
                         "qev_s": q_evts[-1]["sseq"], "nkt": cj, "cj": cj,
                         "smm_last": {-2: smm[(cj - 2, 3)], -1: smm[(cj - 1, 3)]},
                         "exp_last": {-2: expv[cj * 4 - 2], -1: expv[cj * 4 - 1]},
                         "pv_last": {-2: pv_t[cj - 2], -1: pv_t[cj - 1]},
                         "rsT_v": ertv["vseq"]}
            p3.ev["sync"].append((lambda e: None,
                                  [("sx2", p3.now("sx2"))], [], []))
            barrier(p3)
            with nc.Block() as blk3:
                p3.emit(blk3, semh)

        with ExitStack() as es4:
            sb4 = lambda n, s, d=F32: es4.enter_context(nc.sbuf_tensor(n, s, d))
            bufs4 = (sb4("wgt4", [P, DC, II], BF16), sb4("wut4", [P, DC, II], BF16),
                     sb4("wdt4", [P, IC, D], BF16), sb4("xb4", [P, 2, D], F32),
                     sb4("xqb4", [P, D], BF16), sb4("xqtb4", [P, DC, P], BF16),
                     sb4("hb4", [P, II], F32), sb4("hqb4", [P, II], BF16),
                     sb4("hqtb4", [P, IC, P], BF16), sb4("lnreg4", [P, 11, 176], F32),
                     sb4("stg4", [P, DC, 512], BF16))
            p4 = Prog(CNT)
            ffn_phase(p4, x2h, y_out, ("sy0", "sy1"), bufs4, False, AGW)
            p4.ev["sync"].append((lambda e: None,
                                  [("sy0", p4.now("sy0")), ("sy1", p4.now("sy1"))],
                                  [], []))
            with nc.Block() as blk4:
                p4.emit(blk4, semh)
    return nc


_NC_CACHE = None


def kernel(L0_attn_x, L0_attn_Wq, L0_attn_Wk, L0_attn_Wv,
           L0_ffn_w_gate, L0_ffn_w_up, L0_ffn_w_down):
    global _NC_CACHE
    x = np.ascontiguousarray(np.asarray(L0_attn_x, dtype=np.float32))
    wq = np.ascontiguousarray(np.asarray(L0_attn_Wq, dtype=np.float32))
    wk = np.ascontiguousarray(np.asarray(L0_attn_Wk, dtype=np.float32))
    wv = np.ascontiguousarray(np.asarray(L0_attn_Wv, dtype=np.float32))
    wg = np.ascontiguousarray(np.asarray(L0_ffn_w_gate, dtype=np.float32))
    wu = np.ascontiguousarray(np.asarray(L0_ffn_w_up, dtype=np.float32))
    wd = np.ascontiguousarray(np.asarray(L0_ffn_w_down, dtype=np.float32))
    B, T, _ = x.shape

    acat = np.zeros((NAG_A * P, D), np.float32)
    acat[0:II] = wg
    acat[IC * P:IC * P + II] = wu

    in_maps = []
    for c in range(8):
        b, par = c // 2, c % 2
        xb_seq = x[b].reshape(NKB, 512, D)
        x_own = np.concatenate([xb_seq[g] for g in ASSIGN[par]], axis=0)
        dvals = np.zeros(64, np.float32)
        for j in range(QB):
            for tb in range(2):
                kg = CJ[j] - 2 + tb
                for kc in range(4):
                    dvals[(2 * j + tb) * 4 + kc] = BIG * (
                        512.0 * (ASSIGN[par][j] - kg) - 128.0 * kc)
        dtabv = np.broadcast_to(dvals[None, :], (P, 64)).copy()
        wp = np.zeros((NGRP, P, II), np.float32)
        for i in range(6):
            wp[i, :, 0:D] = acat[(6 * c + i) * P:(6 * c + i + 1) * P]
        wp[6] = wd[c * P:(c + 1) * P]
        in_maps.append({"x_in": np.ascontiguousarray(x_own),
                        "wq": wq, "wk": wk, "wv": wv,
                        "wprep": np.ascontiguousarray(wp),
                        "dtab": np.ascontiguousarray(dtabv)})
    if _NC_CACHE is None:
        _NC_CACHE = build()
    res = run_bass_kernel_spmd(_NC_CACHE, in_maps, core_ids=list(range(8)))
    out = np.empty_like(x)
    for c in range(8):
        b, par = c // 2, c % 2
        yv = np.asarray(res.results[c]["y"]).reshape(QB, 512, D)
        for j, g in enumerate(ASSIGN[par]):
            out[b, g * 512:(g + 1) * 512, :] = yv[j]
    return out


# version 04
# speedup vs baseline: 1.1380x; 1.0487x over previous
import numpy as np
import concourse.bass as bass
import concourse.mybir as mybir
from concourse.bass_utils import run_bass_kernel_spmd

F32 = mybir.dt.float32
BF16 = mybir.dt.bfloat16
I32 = mybir.dt.int32
Alu = mybir.AluOpType
Act = mybir.ActivationFunctionType
AX = mybir.AxisListType.X

P = 128
D = 1024
DC = 8
II = 2816
IC = 22
NT = 4096
NTT = 32
NKB = 16
QB = 8
KM = float(1.5 * 2 ** 23)
EPS = 1e-5
BIG = 3000.0
ASSIGN = [[15, 13, 11, 9, 6, 4, 2, 0],
          [14, 12, 10, 8, 7, 5, 3, 1]]
CJ = [16, 14, 12, 10, 8, 6, 4, 2]
SLOT = [0] * NKB
OFF = [0] * NKB
for _par in (0, 1):
    for _j, _g in enumerate(ASSIGN[_par]):
        SLOT[_g] = _par
        OFF[_g] = _j
GCH = [(k * 512, min(512, II - k * 512)) for k in range(6)]
TRND = [(0, 8), (8, 8), (16, 6)]
NGRP = 7
NAG_A = 48

ENGINES = ("sync", "vector", "scalar", "tensor", "gpsimd")
DEBUG_DUMP = False
PROGS = []


class Prog:

    def __init__(self, cnt):
        self.ev = {e: [] for e in ENGINES}
        self.cnt = cnt
        PROGS.append(self)

    def op(self, eng, fn, incs=(), waits=()):
        w = [(s, v) for (s, v) in waits if v > 0]
        self.ev[eng].append((fn, w, list(incs), []))
        for s, n in incs:
            self.cnt[s] = self.cnt.get(s, 0) + n
        return {s: self.cnt[s] for s, _ in incs}

    def now(self, s):
        return self.cnt.get(s, 0)

    def raw(self, eng, fn, bumps=(), waits=()):
        w = [(s, v) for (s, v) in waits if v > 0]
        self.ev[eng].append((fn, w, [], list(bumps)))
        for s, n in bumps:
            self.cnt[s] = self.cnt.get(s, 0) + n
        return {s: self.cnt[s] for s, _ in bumps}

    def emit(self, block, sems):
        def mk(eng):
            def body(eobj):
                for fn, waits, incs, _bumps in self.ev[eng]:
                    for s, v in waits:
                        eobj.wait_ge(sems[s], v)
                    instr = fn(eobj)
                    for s, n in incs:
                        assert instr is not None, f"no instr for inc {s}"
                        instr.then_inc(sems[s], n)
            return body
        block.sync(mk("sync"))
        block.vector(mk("vector"))
        block.scalar(mk("scalar"))
        block.tensor(mk("tensor"))
        block.gpsimd(mk("gpsimd"))


def build():
    nc = bass.Bass("TRN2", target_bir_lowering=False, debug=False,
                   detect_race_conditions=False)
    dt = nc.dram_tensor
    x_in = dt("x_in", [NT, D], F32, kind="ExternalInput").ap()
    wqi = dt("wq", [D, D], F32, kind="ExternalInput").ap()
    wki = dt("wk", [D, D], F32, kind="ExternalInput").ap()
    wvi = dt("wv", [D, D], F32, kind="ExternalInput").ap()
    wprep = dt("wprep", [NGRP, P, II], F32, kind="ExternalInput").ap()
    dtab = dt("dtab", [P, 64], F32, kind="ExternalInput").ap()
    y_out = dt("y", [NT, D], F32, kind="ExternalOutput").ap()
    KD = "ExternalOutput" if DEBUG_DUMP else "Internal"
    x1h = dt("x1h", [NT, D], F32, kind=KD).ap()
    x2h = dt("x2h", [NT, D], F32, kind=KD).ap()
    xaownh = dt("xaownh", [QB, D, 512], BF16, kind=KD).ap()
    srcb = dt("srcb", [QB, D, 512], BF16, kind="Internal").ap()
    bbt = dt("bbt", [QB, 2, D, 512], BF16, kind=KD).ap()
    kth = dt("kth", [NKB, D, 512], BF16, kind=KD).ap()
    vth = dt("vth", [NKB, 512, D], BF16, kind=KD).ap()
    srcA = dt("srcA", [6, D, P], BF16, kind="Internal").ap()
    outA = dt("outA", [NAG_A, D, P], BF16, kind="Internal").ap()
    srcBd = dt("srcBd", [1, II, P], BF16, kind="Internal").ap()
    outB = dt("outB", [8, II, P], BF16, kind="Internal").ap()

    from contextlib import ExitStack
    with ExitStack() as es:
        sbuf = lambda n, s, d=F32: es.enter_context(nc.sbuf_tensor(n, s, d))
        psA = es.enter_context(nc.psum_tensor("psA", [P, 4096], F32))
        ident = sbuf("ident", [P, P], BF16)
        dtb = sbuf("dtb", [P, 64], F32)
        ones = sbuf("ones", [P, 1], BF16)
        qcb = sbuf("qcb", [P, II], F32)
        qqn = sbuf("qqn", [P, II], F32)
        qbm = sbuf("qbm", [P, II // 16], F32)
        qsp = sbuf("qsp", [P, II // 16], F32)
        qrs = sbuf("qrs", [P, II // 16], F32)
        LNW = 176

        def ln_views(lnreg):
            R = {}
            for i, n in enumerate(("s1v", "s2v", "mv", "m2v", "vev", "r0v",
                                   "rsd0", "rsda", "rsdb", "rstdv", "nmrv")):
                R[n] = lnreg[:, i, :]
            R["s1c"] = lnreg[:, 0, 0:1]
            R["s2c"] = lnreg[:, 1, 0:1]
            R["rstdv1"] = lnreg[:, 9, 0:1]
            R["nmrv1"] = lnreg[:, 10, 0:1]
            return R

        SEMS = (["sx0", "sx1", "st0", "st1", "ssrc", "swt", "sy0", "sy1",
                 "sseq", "vseq", "tseq", "gseq", "agc", "swg", "swst",
                 "skv0", "skv1", "skt0", "skt1", "svt0", "svt1", "sxa",
                 "sx1l", "sx2", "gwq", "skst", "bar"])
        semh = {n: es.enter_context(nc.semaphore(n)) for n in SEMS}
        CNT = {n: 0 for n in SEMS}

        ps_gate = [psA[:, 0:512], psA[:, 512:1024]]
        ps_up = [psA[:, 1024:1536], psA[:, 1536:2048]]
        ps_down = [psA[:, 2048:2560], psA[:, 2560:3072]]
        ps_tp = psA[:, 3072:3584].bitcast(BF16)
        ps_tp2 = psA[:, 3584:4096].bitcast(BF16)
        ps_k = [psA[:, 0:512], psA[:, 512:1024]]
        ps_v = [[psA[:, 1024:1536], psA[:, 1536:2048]],
                [psA[:, 2048:2560], psA[:, 2560:3072]]]
        ps_s = [psA[:, 0:512], psA[:, 512:1024]]
        ps_rsv = psA[:, 1024:1536]
        ps_rt = psA[:, 1536:1540]
        ps_q = [psA[:, 0:512], psA[:, 512:1024]]
        ps_o = [[psA[:, (2 * qc + dvh) * 512:(2 * qc + dvh + 1) * 512]
                 for dvh in range(2)] for qc in range(4)]

        def q_pipeline(p, xn_buf, src_ap, dst_ap, w, extra_w_v1=()):
            nb = w // 16
            nbs = II // 16
            xnb = xn_buf[:, 0:w]

            def v1(v):
                v.tensor_reduce(qbm[:, :nbs],
                                xn_buf[:, 0:II].rearrange("p (b k) -> p b k", k=16),
                                AX, Alu.max, apply_absolute_value=True)
                v.tensor_scalar(qsp[:, :nbs], qbm[:, :nbs], 1.0 / 6.0, 1e-12,
                                Alu.mult, Alu.max)
                v.reciprocal(qrs[:, :nbs], qsp[:, :nbs])
                v.tensor_tensor(xnb.rearrange("p (b k) -> p b k", k=16),
                                src_ap.rearrange("p (b k) -> p b k", k=16),
                                qrs[:, :nb].unsqueeze(2).broadcast_to([P, nb, 16]),
                                Alu.mult)
                v.tensor_scalar(qqn[:, :w].bitcast(I32), xnb.bitcast(I32),
                                0x7FFFFFFF, 21, Alu.bitwise_and,
                                Alu.logical_shift_right)
                v.tensor_scalar(qqn[:, :w].bitcast(I32), qqn[:, :w].bitcast(I32),
                                1, 508, Alu.add, Alu.max)
                v.tensor_scalar(qqn[:, :w].bitcast(I32), qqn[:, :w].bitcast(I32),
                                1, 22, Alu.logical_shift_right,
                                Alu.logical_shift_left)
                v.tensor_scalar(qcb[:, :w], xnb, -1.0, 1.0, Alu.max, Alu.min)
                v.tensor_scalar(qcb[:, :w], qcb[:, :w], 2.0, KM,
                                Alu.mult, Alu.add)
                v.tensor_scalar(qcb[:, :w], qcb[:, :w], 0.5, -KM / 2,
                                Alu.mult, Alu.add)
                v.tensor_tensor(qcb[:, :w], qcb[:, :w], qqn[:, :w], Alu.mult)
                return v.tensor_tensor(dst_ap.rearrange("p (b k) -> p b k", k=16),
                                       qcb[:, :w].rearrange("p (b k) -> p b k", k=16),
                                       qsp[:, :nb].unsqueeze(2).broadcast_to(
                                           [P, nb, 16]), Alu.mult)
            e4 = p.op("vector", v1, incs=[("vseq", 1)], waits=list(extra_w_v1))
            return e4, e4

        def ln_pipeline(p, sq_waits, x_ap, n, R):
            def s_sq(s):
                s.activation(qcb[:, 0:n], x_ap, Act.Copy, accum_out=R["s1c"])
                return s.activation(qcb[:, 0:n], x_ap, Act.Square,
                                    accum_out=R["s2c"])
            e1 = p.op("scalar", s_sq, incs=[("sseq", 1)], waits=list(sq_waits))

            def v_st(v):
                v.tensor_scalar(R["mv"], R["s1v"], 1.0 / n, None, Alu.mult)
                v.tensor_tensor(R["m2v"], R["mv"], R["mv"], Alu.mult)
                v.tensor_scalar(R["vev"], R["s2v"], 1.0 / n, EPS,
                                Alu.mult, Alu.add)
                v.tensor_tensor(R["vev"], R["vev"], R["m2v"], Alu.subtract)
                v.tensor_scalar(R["r0v"], R["vev"], 0.5, None, Alu.mult)
                v.tensor_scalar(R["rsd0"].bitcast(I32), R["vev"].bitcast(I32),
                                1, None, Alu.logical_shift_right)
                v.tensor_scalar(R["rsd0"].bitcast(I32), R["rsd0"].bitcast(I32),
                                0xFFFFFFFF, None, Alu.bitwise_xor)
                v.tensor_scalar(R["rsd0"].bitcast(I32), R["rsd0"].bitcast(I32),
                                0x5f3759df + 1, None, Alu.add)
                for it in range(3):
                    last = it == 2
                    v.tensor_tensor(R["rsda"], R["rsd0"], R["rsd0"], Alu.mult)
                    v.tensor_tensor(R["rsda"], R["r0v"], R["rsda"], Alu.mult)
                    v.tensor_scalar(R["rsda"], R["rsda"], -1.0, 1.5,
                                    Alu.mult, Alu.add)
                    v.tensor_tensor(R["rstdv"] if last else R["rsd0"],
                                    R["rsd0"], R["rsda"], Alu.mult)
                return v.scalar_tensor_tensor(R["nmrv"], R["mv"], -1.0, R["rstdv"],
                                              Alu.mult, Alu.mult)
            return p.op("vector", v_st, incs=[("vseq", 1)],
                        waits=[("sseq", e1["sseq"])])

        def barrier(p):
            v = p.now("bar") + len(ENGINES)
            for e in ENGINES:
                p.op(e, lambda eng: eng.nop(), incs=[("bar", 1)])
            for e in ENGINES:
                p.ev[e].append((lambda eng: None, [("bar", v)], [], []))

        with ExitStack() as es0:
            sb0 = lambda n, s, d=F32: es0.enter_context(nc.sbuf_tensor(n, s, d))
            wgb = sb0("wgb", [P, II], F32)
            wqg = sb0("wqg", [P, II], BF16)
            wstg = sb0("wstg", [P, IC, P], BF16)
            iof = sb0("iof", [P, P], F32)
            p0 = Prog(CNT)
            p0.op("sync", lambda e: e.dma_start(out=dtb[:], in_=dtab[:]),
                  incs=[("swg", 16)])

            def g_iota(g):
                return g.iota(iof[:], [[1, P]], base=0, channel_multiplier=-1,
                              allow_small_or_imprecise_dtypes=True)
            eio = p0.op("gpsimd", g_iota, incs=[("gseq", 1)])

            def v_id(v):
                v.memset(ones[:], 1.0)
                return v.tensor_scalar(ident[:], iof[:], 0.0, None, Alu.is_equal)
            p0.op("vector", v_id, incs=[("vseq", 1)], waits=[("gseq", eio["gseq"])])

            for i in range(NGRP):
                el = p0.op("sync",
                           lambda e, i=i: e.dma_start(out=wgb[:], in_=wprep[i]),
                           incs=[("swg", 16)], waits=[("vseq", p0.now("vseq"))])
                eq, _ = q_pipeline(p0, wgb, wgb[:, :], wqg[:, :], II,
                                   extra_w_v1=[("swg", el["swg"])])
                last_evac = None
                for (i0, ntp) in TRND:
                    def t_tp(t, i0=i0, ntp=ntp):
                        r = None
                        for k in range(ntp):
                            r = t.transpose(ps_tp[:, k * P:(k + 1) * P],
                                            wqg[:, (i0 + k) * P:(i0 + k + 1) * P],
                                            ident[:])
                        return r
                    wts = [("vseq", eq["vseq"])]
                    if last_evac is not None:
                        wts.append(("vseq", last_evac["vseq"]))
                    ett = p0.op("tensor", t_tp, incs=[("tseq", 1)], waits=wts)

                    def v_ev(v, i0=i0, ntp=ntp):
                        return v.tensor_copy(
                            wstg[:, i0:i0 + ntp, :],
                            ps_tp[:, 0:ntp * P].rearrange("p (c q) -> p c q", q=P))
                    last_evac = p0.op("vector", v_ev, incs=[("vseq", 1)],
                                      waits=[("tseq", ett["tseq"])])
                if i < 6:
                    def st(e, i=i):
                        return e.dma_start(
                            out=srcA[i].rearrange("(c p) q -> p c q", p=P),
                            in_=wstg[:, 0:DC, :])
                else:
                    def st(e):
                        return e.dma_start(
                            out=srcBd[0].rearrange("(c p) q -> p c q", p=P),
                            in_=wstg[:, :, :])
                p0.op("sync", st, incs=[("swst", 16)],
                      waits=[("vseq", last_evac["vseq"])])

            def ag_a(g):
                return g.collective_compute(
                    "AllGather", Alu.bypass, replica_groups=[list(range(8))],
                    ins=[srcA[:].rearrange("g d p -> (g d) p")],
                    outs=[outA[:].rearrange("g d p -> (g d) p")])
            ea = p0.op("gpsimd", ag_a, incs=[("agc", 1)],
                       waits=[("swst", p0.now("swst"))])
            p0.ev["gpsimd"].append((lambda g: None, [("agc", ea["agc"])], [], []))

            def ag_b(g):
                return g.collective_compute(
                    "AllGather", Alu.bypass, replica_groups=[list(range(8))],
                    ins=[srcBd[:].rearrange("g d p -> (g d) p")],
                    outs=[outB[:].rearrange("g d p -> (g d) p")])
            eb = p0.op("gpsimd", ag_b, incs=[("agc", 1)])
            p0.ev["gpsimd"].append((lambda g: None, [("agc", eb["agc"])], [], []))
            AGW = eb["agc"]

            with nc.Block() as blk0:
                p0.emit(blk0, semh)

        def ffn_phase(p, src_hbm, dst_hbm, st_sems, bufs, with_ln2, agw):
            (wgt, wut, wdt, xb, xqb, xqtb, hb, hqb, hqtb, lnreg, stg) = bufs
            R = ln_views(lnreg)
            silub = [qcb[:, 0:512], qcb[:, 512:1024]]

            def v_init(v):
                v.memset(hb[:, :], 1.0)
                return v.memset(lnreg[:].rearrange("p a b -> p (a b)"), 1.0)
            p.op("vector", v_init, incs=[("vseq", 1)])

            swth = semh["swt"]

            def wload(e):
                for g in range(IC):
                    e.dma_start(out=wgt[:, :, g * P:(g + 1) * P],
                                in_=outA[g].rearrange("(c p) q -> p c q", p=P)
                                ).then_inc(swth, 16)
                    e.dma_start(out=wut[:, :, g * P:(g + 1) * P],
                                in_=outA[IC + g].rearrange("(c p) q -> p c q", p=P)
                                ).then_inc(swth, 16)
                for g in range(DC):
                    e.dma_start(out=wdt[:, :, g * P:(g + 1) * P],
                                in_=outB[g].rearrange("(c p) q -> p c q", p=P)
                                ).then_inc(swth, 16)
                return None
            ewl = p.raw("sync", wload, bumps=[("swt", 16 * (2 * IC + DC))],
                        waits=[("agc", agw)])

            prev = {}
            for tt in range(NTT):
                par = tt % 2
                sxn = "sx0" if par == 0 else "sx1"
                stn = st_sems[par]
                xtile = xb[:, par, :]
                lw = []
                if tt >= 2:
                    q = prev[tt - 2]
                    lw = [(stn, q["store"]), ("sseq", q["lastread_s"]),
                          ("vseq", q["lastread_v"])]
                ex = p.op("sync",
                          lambda e, tt=tt, xtile=xtile: e.dma_start(
                              out=xtile, in_=src_hbm[tt * P:(tt + 1) * P, :]),
                          incs=[(sxn, 16)], waits=lw)
                sq_w = [(sxn, ex[sxn])]
                if tt >= 1:
                    sq_w += [("vseq", prev[tt - 1]["ln2stats_v"]),
                             ("vseq", prev[tt - 1]["qcb_free_g"])]
                eln = ln_pipeline(p, sq_w, xtile, D, R)

                def s_norm(s, xtile=xtile):
                    return s.activation(hb[:, 0:D], xtile, Act.Identity,
                                        bias=R["nmrv1"], scale=R["rstdv1"])
                en = p.op("scalar", s_norm, incs=[("sseq", 1)],
                          waits=[("vseq", eln["vseq"])])
                eqa, _ = q_pipeline(p, hb, hb[:, 0:D], xqb[:, :], D,
                                    extra_w_v1=[("sseq", en["sseq"])])

                def t_tpq(t):
                    r = None
                    for k in range(DC):
                        r = t.transpose(ps_tp[:, k * P:(k + 1) * P],
                                        xqb[:, k * P:(k + 1) * P], ident[:])
                    return r
                etq = p.op("tensor", t_tpq, incs=[("tseq", 1)],
                           waits=[("vseq", eqa["vseq"])])

                def v_evq(v):
                    return v.tensor_copy(xqtb[:],
                                         ps_tp[:].rearrange("p (c q) -> p c q", q=P))
                eevq = p.op("vector", v_evq, incs=[("vseq", 1)],
                            waits=[("tseq", etq["tseq"])])

                hchunk_v = []
                for k, (lo, w) in enumerate(GCH):
                    wts = [("vseq", eevq["vseq"]), ("swt", ewl["swt"])]
                    if k >= 2:
                        wts.append(("vseq", hchunk_v[k - 2]["vseq"]))
                    elif tt >= 1:
                        wts.append(("vseq", prev[tt - 1]["hch_v"][4 + k]))

                    def t_gu(t, lo=lo, w=w, k=k):
                        r = None
                        for dc in range(DC):
                            t.matmul(ps_gate[k % 2][:, 0:w], xqtb[:, dc, :],
                                     wgt[:, dc, lo:lo + w],
                                     start=(dc == 0), stop=(dc == DC - 1))
                            r = t.matmul(ps_up[k % 2][:, 0:w], xqtb[:, dc, :],
                                         wut[:, dc, lo:lo + w],
                                         start=(dc == 0), stop=(dc == DC - 1))
                        return r
                    etg = p.op("tensor", t_gu, incs=[("tseq", 1)], waits=wts)

                    def s_si(s, w=w, k=k):
                        return s.activation(silub[k % 2][:, 0:w],
                                            ps_gate[k % 2][:, 0:w], Act.Silu)
                    esi = p.op("scalar", s_si, incs=[("sseq", 1)],
                               waits=[("tseq", etg["tseq"])])

                    def v_h(v, lo=lo, w=w, k=k):
                        return v.tensor_tensor(hb[:, lo:lo + w], ps_up[k % 2][:, 0:w],
                                               silub[k % 2][:, 0:w], Alu.mult)
                    ehm = p.op("vector", v_h, incs=[("vseq", 1)],
                               waits=[("sseq", esi["sseq"])])
                    hchunk_v.append(ehm)
                hch_vals = [e["vseq"] for e in hchunk_v]

                HSP = 1408
                HCH = HSP // P

                def q_elem(v, lo, w):
                    xnb = hb[:, lo:lo + w]
                    nb0, nb1 = lo // 16, (lo + w) // 16
                    nbw = nb1 - nb0
                    v.tensor_tensor(xnb.rearrange("p (b k) -> p b k", k=16),
                                    hb[:, lo:lo + w].rearrange(
                                        "p (b k) -> p b k", k=16),
                                    qrs[:, nb0:nb1].unsqueeze(2).broadcast_to(
                                        [P, nbw, 16]), Alu.mult)
                    v.tensor_scalar(qqn[:, lo:lo + w].bitcast(I32),
                                    xnb.bitcast(I32), 0x7FFFFFFF, 21,
                                    Alu.bitwise_and, Alu.logical_shift_right)
                    v.tensor_scalar(qqn[:, lo:lo + w].bitcast(I32),
                                    qqn[:, lo:lo + w].bitcast(I32),
                                    1, 508, Alu.add, Alu.max)
                    v.tensor_scalar(qqn[:, lo:lo + w].bitcast(I32),
                                    qqn[:, lo:lo + w].bitcast(I32),
                                    1, 22, Alu.logical_shift_right,
                                    Alu.logical_shift_left)
                    v.tensor_scalar(qcb[:, lo:lo + w], xnb, -1.0, 1.0,
                                    Alu.max, Alu.min)
                    v.tensor_scalar(qcb[:, lo:lo + w], qcb[:, lo:lo + w],
                                    2.0, KM, Alu.mult, Alu.add)
                    v.tensor_scalar(qcb[:, lo:lo + w], qcb[:, lo:lo + w],
                                    0.5, -KM / 2, Alu.mult, Alu.add)
                    v.tensor_tensor(qcb[:, lo:lo + w], qcb[:, lo:lo + w],
                                    qqn[:, lo:lo + w], Alu.mult)
                    return v.tensor_tensor(
                        hqb[:, lo:lo + w].rearrange("p (b k) -> p b k", k=16),
                        qcb[:, lo:lo + w].rearrange("p (b k) -> p b k", k=16),
                        qsp[:, nb0:nb1].unsqueeze(2).broadcast_to([P, nbw, 16]),
                        Alu.mult)

                def v_hq1(v):
                    v.tensor_reduce(qbm[:],
                                    hb[:, 0:II].rearrange("p (b k) -> p b k", k=16),
                                    AX, Alu.max, apply_absolute_value=True)
                    v.tensor_scalar(qsp[:], qbm[:], 1.0 / 6.0, 1e-12,
                                    Alu.mult, Alu.max)
                    v.reciprocal(qrs[:], qsp[:])
                    return q_elem(v, 0, HSP)
                eqh1 = p.op("vector", v_hq1, incs=[("vseq", 1)])
                eqh2 = p.op("vector", lambda v: q_elem(v, HSP, II - HSP),
                            incs=[("vseq", 1)])
                egh = eqh2

                evac_s = None
                evac_half = [None, None]
                for (i0, ntp, half) in ((0, 8, 0), (8, 3, 0),
                                        (11, 8, 1), (19, 3, 1)):
                    wts = [("vseq", (eqh1 if half == 0 else eqh2)["vseq"])]
                    if evac_s is not None:
                        wts.append(("sseq", evac_s["sseq"]))
                    elif tt >= 1:
                        wts.append(("sseq", prev[tt - 1]["hqev_s"]))

                    def t_th(t, i0=i0, ntp=ntp):
                        r = None
                        for k in range(ntp):
                            r = t.transpose(ps_tp2[:, k * P:(k + 1) * P],
                                            hqb[:, (i0 + k) * P:(i0 + k + 1) * P],
                                            ident[:])
                        return r
                    eth = p.op("tensor", t_th, incs=[("tseq", 1)], waits=wts)

                    def s_ev(s, i0=i0, ntp=ntp):
                        return s.copy(hqtb[:, i0:i0 + ntp, :],
                                      ps_tp2[:, 0:ntp * P].rearrange(
                                          "p (c q) -> p c q", q=P))
                    evac_s = p.op("scalar", s_ev, incs=[("sseq", 1)],
                                  waits=[("tseq", eth["tseq"])])
                    evac_half[half] = evac_s
                    if i0 == 8:
                        dwts = [("sseq", evac_s["sseq"])]
                        if tt >= 1:
                            dwts.append(("vseq", prev[tt - 1]["x1_v"]))

                        def t_dn_a(t):
                            r = None
                            for ic in range(HCH):
                                t.matmul(ps_down[0], hqtb[:, ic, :],
                                         wdt[:, ic, 0:512],
                                         start=(ic == 0), stop=False)
                                r = t.matmul(ps_down[1], hqtb[:, ic, :],
                                             wdt[:, ic, 512:D],
                                             start=(ic == 0), stop=False)
                            return r
                        p.op("tensor", t_dn_a, incs=[("tseq", 1)], waits=dwts)

                def t_dn(t):
                    r = None
                    for ic in range(HCH, IC):
                        t.matmul(ps_down[0], hqtb[:, ic, :], wdt[:, ic, 0:512],
                                 start=False, stop=(ic == IC - 1))
                        r = t.matmul(ps_down[1], hqtb[:, ic, :], wdt[:, ic, 512:D],
                                     start=False, stop=(ic == IC - 1))
                    return r
                edn = p.op("tensor", t_dn, incs=[("tseq", 1)],
                           waits=[("sseq", evac_half[1]["sseq"])])

                def v_x1(v, xtile=xtile):
                    v.scalar_tensor_tensor(xtile[:, 0:512], ps_down[0], 0.5,
                                           xtile[:, 0:512], Alu.mult, Alu.add)
                    return v.scalar_tensor_tensor(xtile[:, 512:D], ps_down[1], 0.5,
                                                  xtile[:, 512:D], Alu.mult, Alu.add)
                ex1 = p.op("vector", v_x1, incs=[("vseq", 1)],
                           waits=[("tseq", edn["tseq"])])
                est = p.op("sync",
                           lambda e, tt=tt, xtile=xtile: e.dma_start(
                               out=dst_hbm[tt * P:(tt + 1) * P, :], in_=xtile),
                           incs=[(stn, 16)], waits=[("vseq", ex1["vseq"])])
                rec = {"store": est[stn], "x1_v": ex1["vseq"], "hch_v": hch_vals,
                       "hqev_s": evac_s["sseq"], "qcb_free_g": egh["vseq"]}

                if with_ln2:
                    eln2 = ln_pipeline(p, [("vseq", ex1["vseq"]),
                                           ("vseq", egh["vseq"])], xtile, D, R)

                    def s_xa(s, xtile=xtile):
                        return s.activation(xqb[:], xtile, Act.Identity,
                                            bias=R["nmrv1"], scale=R["rstdv1"])
                    exa = p.op("scalar", s_xa, incs=[("sseq", 1)],
                               waits=[("vseq", eln2["vseq"])])

                    def t_ta(t):
                        r = None
                        for k in range(DC):
                            r = t.transpose(ps_tp[:, k * P:(k + 1) * P],
                                            xqb[:, k * P:(k + 1) * P], ident[:])
                        return r
                    eta = p.op("tensor", t_ta, incs=[("tseq", 1)],
                               waits=[("sseq", exa["sseq"])])
                    rnd, sl = tt // 4, tt % 4

                    def v_ea(v, sl=sl):
                        return v.tensor_copy(
                            stg[:, :, sl * P:(sl + 1) * P],
                            ps_tp[:].rearrange("p (c q) -> p c q", q=P))
                    wts = [("tseq", eta["tseq"])]
                    if sl == 0 and rnd >= 1:
                        wts.append(("ssrc", 32 * rnd))
                    eea = p.op("vector", v_ea, incs=[("vseq", 1)], waits=wts)
                    rec["ln2stats_v"] = eln2["vseq"]
                    if sl == 3:
                        ssrch = semh["ssrc"]

                        def st_src(e, rnd=rnd):
                            e.dma_start(
                                out=srcb[rnd].rearrange("(c p) t -> p c t", p=P),
                                in_=stg[:]).then_inc(ssrch, 16)
                            e.dma_start(
                                out=xaownh[rnd].rearrange("(c p) t -> p c t", p=P),
                                in_=stg[:]).then_inc(ssrch, 16)
                            return None
                        esrc = p.raw("sync", st_src, bumps=[("ssrc", 32)],
                                     waits=[("vseq", eea["vseq"])])

                        def g_ag(g, rnd=rnd):
                            return g.collective_compute(
                                "AllGather", Alu.bypass,
                                replica_groups=[[0, 1], [2, 3], [4, 5], [6, 7]],
                                ins=[srcb[rnd]],
                                outs=[bbt[rnd].rearrange("s d t -> (s d) t")])
                        eag = p.op("gpsimd", g_ag, incs=[("agc", 1)],
                                   waits=[("ssrc", esrc["ssrc"])])
                        p.ev["gpsimd"].append(
                            (lambda g: None, [("agc", eag["agc"])], [], []))
                else:
                    rec["ln2stats_v"] = ex1["vseq"]
                rec["lastread_s"] = p.now("sseq")
                rec["lastread_v"] = p.now("vseq")
                prev[tt] = rec
            p.ev["sync"].append((lambda e: None,
                                 [(st_sems[0], p.now(st_sems[0])),
                                  (st_sems[1], p.now(st_sems[1]))], [], []))

        with ExitStack() as es1:
            sb1 = lambda n, s, d=F32: es1.enter_context(nc.sbuf_tensor(n, s, d))
            bufs1 = (sb1("wgt", [P, DC, II], BF16), sb1("wut", [P, DC, II], BF16),
                     sb1("wdt", [P, IC, D], BF16), sb1("xb", [P, 2, D], F32),
                     sb1("xqb", [P, D], BF16), sb1("xqtb", [P, DC, P], BF16),
                     sb1("hb", [P, II], F32), sb1("hqb", [P, II], BF16),
                     sb1("hqtb", [P, IC, P], BF16), sb1("lnreg1", [P, 11, 176], F32),
                     sb1("stg", [P, DC, 512], BF16))
            p1 = Prog(CNT)
            ffn_phase(p1, x_in, x1h, ("st0", "st1"), bufs1, True, AGW)
            AGXA = p1.now("agc")
            barrier(p1)
            with nc.Block() as blk1:
                p1.emit(blk1, semh)

        with ExitStack() as es3:
            sb3 = lambda n, s, d=F32: es3.enter_context(nc.sbuf_tensor(n, s, d))
            wq16 = sb3("wq16", [P, DC, D], BF16)
            wk16 = sb3("wk16", [P, DC, D], BF16)
            wv16 = sb3("wv16", [P, DC, D], BF16)
            qtb = sb3("qtb", [P, DC, 512], BF16)
            xaq = sb3("xaq", [P, DC, 512], BF16)
            ktb = sb3("ktb", [P, 2, DC, 512], BF16)
            vb = sb3("vb", [P, 2, 4, D], BF16)
            ptb = sb3("ptb", [P, 64, 512], BF16)
            x1j = sb3("x1j", [P, 4, D], F32)
            d3 = sb3("d3", [P, 512], F32)
            mskb = sb3("mskb", [P, 512], F32)
            rsTin = mskb[:, 0:LNW]
            rsTw = mskb[:, 256:256 + LNW]
            rse = sb3("rse", [1, 512], F32)
            onesf = sb3("onesf", [1, 1], F32)
            p3 = Prog(CNT)

            def g_d3(g):
                return g.iota(d3[:], [[int(BIG), 512]], base=0,
                              channel_multiplier=-int(BIG),
                              allow_small_or_imprecise_dtypes=True)
            p3.op("gpsimd", g_d3, incs=[("gseq", 1)])
            p3.op("vector", lambda v: v.memset(onesf[:], 1.0), incs=[("vseq", 1)])

            gwqh = semh["gwq"]

            def g_cast(g):
                g.dma_start(out=wq16[:],
                            in_=wqi[:].rearrange("(c p) d -> p c d", p=P)
                            ).then_inc(gwqh, 16)
                g.dma_start(out=wk16[:],
                            in_=wki[:].rearrange("(c p) d -> p c d", p=P)
                            ).then_inc(gwqh, 16)
                g.dma_start(out=wv16[:],
                            in_=wvi[:].rearrange("(c p) d -> p c d", p=P)
                            ).then_inc(gwqh, 16)
                return None
            egc = p3.raw("gpsimd", g_cast, bumps=[("gwq", 48)])

            kst = {}
            for e2 in range(NKB):
                kg = NKB - 1 - e2
                par = e2 % 2
                skvn = "skv0" if par == 0 else "skv1"
                lwts = [("agc", AGXA - QB + OFF[kg] + 1)]
                if e2 >= 2:
                    lwts.append(("tseq", kst[e2 - 2]["t_end"]))

                def l_kv(e, kg=kg, par=par):
                    return e.dma_start(
                        out=ktb[:, par, :, :],
                        in_=bbt[OFF[kg], SLOT[kg]].rearrange("(c p) t -> p c t", p=P))
                ekl = p3.op("sync", l_kv, incs=[(skvn, 16)], waits=lwts)

                ks_evts = []
                for dkc in range(DC):
                    wts = [(skvn, ekl[skvn]), ("gwq", egc["gwq"])]
                    if dkc >= 2:
                        wts.append(("sseq", ks_evts[dkc - 2]["sseq"]))
                    elif e2 >= 1:
                        wts.append(("sseq", kst[e2 - 1]["kev_s"]))

                    def t_k(t, dkc=dkc, par=par):
                        r = None
                        for dc in range(DC):
                            r = t.matmul(ps_k[dkc % 2],
                                         wk16[:, dc, dkc * P:(dkc + 1) * P],
                                         ktb[:, par, dc, :],
                                         start=(dc == 0), stop=(dc == DC - 1))
                        return r
                    etk = p3.op("tensor", t_k, incs=[("tseq", 1)], waits=wts)
                    sw = [("tseq", etk["tseq"])]
                    if e2 >= 1 and dkc == 0:
                        sw.append(("skst", kst[e2 - 1]["kst"]))

                    def s_ke(s, dkc=dkc):
                        return s.copy(ptb[:, dkc, :], ps_k[dkc % 2])
                    eke = p3.op("scalar", s_ke, incs=[("sseq", 1)], waits=sw)
                    ks_evts.append(eke)
                ekst = p3.op("sync",
                             lambda e, kg=kg: e.dma_start(
                                 out=kth[kg].rearrange("(c p) t -> p c t", p=P),
                                 in_=ptb[:, 0:DC, :]),
                             incs=[("skst", 16)],
                             waits=[("sseq", ks_evts[-1]["sseq"])])

                vs_evts = []
                for tc in range(4):
                    wts = []
                    if tc >= 2:
                        wts.append(("vseq", vs_evts[tc - 2]["vseq"]))
                    elif e2 >= 1:
                        wts.append(("vseq", kst[e2 - 1]["vev_v"]))

                    def t_v(t, tc=tc, par=par):
                        r = None
                        for dc in range(DC):
                            t.matmul(ps_v[tc % 2][0],
                                     ktb[:, par, dc, tc * P:(tc + 1) * P],
                                     wv16[:, dc, 0:512],
                                     start=(dc == 0), stop=(dc == DC - 1))
                            r = t.matmul(ps_v[tc % 2][1],
                                         ktb[:, par, dc, tc * P:(tc + 1) * P],
                                         wv16[:, dc, 512:D],
                                         start=(dc == 0), stop=(dc == DC - 1))
                        return r
                    etv = p3.op("tensor", t_v, incs=[("tseq", 1)], waits=wts)
                    vw = [("tseq", etv["tseq"])]
                    if e2 >= 1 and tc == 0:
                        vw.append(("skst", kst[e2 - 1]["vst"]))

                    def v_ve(v, tc=tc):
                        v.tensor_copy(ptb[:, 8 + 2 * tc, :], ps_v[tc % 2][0])
                        return v.tensor_copy(ptb[:, 9 + 2 * tc, :], ps_v[tc % 2][1])
                    eve = p3.op("vector", v_ve, incs=[("vseq", 1)], waits=vw)
                    vs_evts.append(eve)
                evst = p3.op("sync",
                             lambda e, kg=kg: e.dma_start(
                                 out=vth[kg].rearrange("(tc p) d -> p tc d", p=P),
                                 in_=ptb[:, 8:16, :].rearrange(
                                     "p (tc h) t -> p tc (h t)", h=2)),
                             incs=[("skst", 16)],
                             waits=[("vseq", vs_evts[-1]["vseq"])])
                kst[e2] = {"t_end": p3.now("tseq"), "kev_s": ks_evts[-1]["sseq"],
                           "vev_v": vs_evts[-1]["vseq"], "kst": ekst["skst"],
                           "vst": evst["skst"]}
            P3A_SKST = p3.now("skst")
            P3A_TSEQ = p3.now("tseq")

            prevj = None
            for j in range(QB):
                cj = CJ[j]
                lw = [("skst", P3A_SKST)]
                if j >= 1:
                    lw.append(("tseq", prevj["q_end"]))
                exq = p3.op("sync",
                            lambda e, j=j: e.dma_start(
                                out=xaq[:],
                                in_=xaownh[j].rearrange("(c p) t -> p c t", p=P)),
                            incs=[("sxa", 16)], waits=lw)
                lw2 = [("sx2", prevj["x2st"])] if j >= 1 else []
                ex1l = p3.op("sync",
                             lambda e, j=j: e.dma_start(
                                 out=x1j[:],
                                 in_=x1h[j * 512:(j + 1) * 512, :].rearrange(
                                     "(q p) d -> p q d", p=P)),
                             incs=[("sx1l", 16)], waits=lw2)

                q_evts = []
                for dkc in range(DC):
                    wts = [("sxa", exq["sxa"]), ("gwq", egc["gwq"])]
                    if dkc >= 2:
                        wts.append(("sseq", q_evts[dkc - 2]["sseq"]))
                    if dkc == 0 and j >= 1:
                        wts.append(("vseq", prevj["x2_v"]))

                    def t_q(t, dkc=dkc):
                        r = None
                        for dc in range(DC):
                            r = t.matmul(ps_q[dkc % 2],
                                         wq16[:, dc, dkc * P:(dkc + 1) * P],
                                         xaq[:, dc, :],
                                         start=(dc == 0), stop=(dc == DC - 1))
                        return r
                    etq2 = p3.op("tensor", t_q, incs=[("tseq", 1)], waits=wts)

                    def s_qe(s, dkc=dkc):
                        return s.copy(qtb[:, dkc, :], ps_q[dkc % 2])
                    eqe = p3.op("scalar", s_qe, incs=[("sseq", 1)],
                                waits=[("tseq", etq2["tseq"])])
                    q_evts.append(eqe)
                q_end = p3.now("tseq")

                smm = {}
                expv = {}
                ktload = {}
                for kg in range(cj):
                    sktn = "skt0" if kg % 2 == 0 else "skt1"
                    lkw = [("tseq", P3A_TSEQ if j == 0 and kg < 2 else 0)]
                    if kg >= 2:
                        lkw.append(("tseq", smm[(kg - 2, 3)]))
                    elif j >= 1 and prevj["nkt"] >= 2 - kg:
                        lkw.append(("tseq", prevj["smm_last"][kg - 2]))
                    ekt = p3.op("sync",
                                lambda e, kg=kg: e.dma_start(
                                    out=ktb[:, kg % 2, :, :],
                                    in_=kth[kg].rearrange("(c p) t -> p c t", p=P)),
                                incs=[(sktn, 16)], waits=lkw)
                    ktload[kg] = ekt[sktn]
                    for kc in range(4):
                        slot = kg * 4 + kc
                        wts = [(sktn, ektl) for ektl in [ktload[kg]]]
                        wts.append(("sseq", q_evts[-1]["sseq"]))
                        if slot >= 2:
                            wts.append(("sseq", expv[slot - 2]))
                        elif j >= 1:
                            wts.append(("sseq", prevj["exp_last"][slot - 2]))
                        if slot == 0:
                            wts.append(("vseq", prevj["rsT_v"] if j >= 1 else 0))

                        def t_s(t, kg=kg, kc=kc):
                            r = None
                            for dc in range(DC):
                                r = t.matmul(ps_s[kc % 2],
                                             ktb[:, kg % 2, dc, kc * P:(kc + 1) * P],
                                             qtb[:, dc, :],
                                             start=(dc == 0), stop=(dc == DC - 1))
                            return r
                        ets = p3.op("tensor", t_s, incs=[("tseq", 1)], waits=wts)
                        smm[(kg, kc)] = ets["tseq"]

                        if kg >= cj - 2:
                            tb = kg - (cj - 2)

                            def v_m(v, kc=kc, tb=tb, j=j):
                                ix = (2 * j + tb) * 4 + kc
                                v.tensor_scalar(mskb[:], d3[:],
                                                dtb[:, ix:ix + 1],
                                                0.0, Alu.add, Alu.min)
                                return v.tensor_tensor(ps_s[kc % 2], ps_s[kc % 2],
                                                       mskb[:], Alu.add)
                            evm = p3.op("vector", v_m, incs=[("vseq", 1)],
                                        waits=[("tseq", ets["tseq"])])
                            ewait = [("vseq", evm["vseq"])]
                        else:
                            ewait = [("tseq", ets["tseq"])]

                        def s_e(s, slot=slot, kc=kc):
                            return s.activation(ptb[:, slot, :], ps_s[kc % 2],
                                                Act.Exp, bias=0.0, scale=1.0 / 32.0)
                        ese = p3.op("scalar", s_e, incs=[("sseq", 1)], waits=ewait)
                        expv[slot] = ese["sseq"]

                        def t_rs(t, slot=slot, cj=cj):
                            return t.matmul(ps_rsv[0:1, 0:512], ones[:],
                                            ptb[:, slot, :], start=(slot == 0),
                                            stop=(slot == cj * 4 - 1))
                        p3.op("tensor", t_rs, incs=[("tseq", 1)],
                              waits=[("sseq", ese["sseq"])])

                def v_rs(v):
                    return v.tensor_scalar(rse[:], ps_rsv[0:1, 0:512], 1e-30, None,
                                           Alu.add)
                ers = p3.op("vector", v_rs, incs=[("vseq", 1)],
                            waits=[("tseq", p3.now("tseq"))])

                def t_rt(t):
                    r = None
                    for qc in range(4):
                        r = t.matmul(ps_rt[:, qc:qc + 1],
                                     rse[0:1, qc * P:(qc + 1) * P], onesf[:],
                                     start=True, stop=True)
                    return r
                ert = p3.op("tensor", t_rt, incs=[("tseq", 1)],
                            waits=[("vseq", ers["vseq"])])

                ems = p3.op("vector", lambda v: v.memset(rsTin, 1.0),
                            incs=[("vseq", 1)])

                def s_rt2(s):
                    return s.copy(rsTin[:, 0:4], ps_rt[:])
                ecrt = p3.op("scalar", s_rt2, incs=[("sseq", 1)],
                             waits=[("tseq", ert["tseq"]), ("vseq", ems["vseq"])])

                def v_rt(v):
                    return v.reciprocal(rsTw, rsTin)
                ertv = p3.op("vector", v_rt, incs=[("vseq", 1)],
                             waits=[("sseq", ecrt["sseq"])])

                pv_t = {}
                for kg in range(cj):
                    svtn = "svt0" if kg % 2 == 0 else "svt1"
                    vw = []
                    if kg >= 2:
                        vw.append(("tseq", pv_t[kg - 2]))
                    elif j >= 1 and kg - 2 + prevj["cj"] >= 0:
                        vw.append(("tseq", prevj["pv_last"][kg - 2]))
                    evl = p3.op("sync",
                                lambda e, kg=kg: e.dma_start(
                                    out=vb[:, kg % 2, :, :],
                                    in_=vth[kg].rearrange("(tc p) d -> p tc d", p=P)),
                                incs=[(svtn, 16)], waits=vw)

                    wts = [(svtn, evl[svtn]), ("sseq", expv[cj * 4 - 1]),
                           ("vseq", ertv["vseq"])]

                    def t_pv(t, kg=kg, cj=cj):
                        r = None
                        for tc in range(4):
                            for qc in range(4):
                                for dvh in range(2):
                                    r = t.matmul(
                                        ps_o[qc][dvh],
                                        ptb[:, kg * 4 + tc, qc * P:(qc + 1) * P],
                                        vb[:, kg % 2, tc, dvh * 512:(dvh + 1) * 512],
                                        start=(kg == 0 and tc == 0),
                                        stop=(kg == cj - 1 and tc == 3))
                        return r
                    etpv = p3.op("tensor", t_pv, incs=[("tseq", 1)], waits=wts)
                    pv_t[kg] = etpv["tseq"]

                def v_x2(v):
                    r = None
                    for qc in range(4):
                        for dvh in range(2):
                            r = v.scalar_tensor_tensor(
                                x1j[:, qc, dvh * 512:(dvh + 1) * 512],
                                ps_o[qc][dvh], rsTw[:, qc:qc + 1],
                                x1j[:, qc, dvh * 512:(dvh + 1) * 512],
                                Alu.mult, Alu.add)
                    return r
                ex2 = p3.op("vector", v_x2, incs=[("vseq", 1)],
                            waits=[("tseq", pv_t[cj - 1]), ("sx1l", ex1l["sx1l"])])
                ex2s = p3.op("sync",
                             lambda e, j=j: e.dma_start(
                                 out=x2h[j * 512:(j + 1) * 512, :].rearrange(
                                     "(q p) d -> p q d", p=P),
                                 in_=x1j[:]),
                             incs=[("sx2", 16)], waits=[("vseq", ex2["vseq"])])
                prevj = {"q_end": q_end, "x2st": ex2s["sx2"], "x2_v": ex2["vseq"],
                         "qev_s": q_evts[-1]["sseq"], "nkt": cj, "cj": cj,
                         "smm_last": {-2: smm[(cj - 2, 3)], -1: smm[(cj - 1, 3)]},
                         "exp_last": {-2: expv[cj * 4 - 2], -1: expv[cj * 4 - 1]},
                         "pv_last": {-2: pv_t[cj - 2], -1: pv_t[cj - 1]},
                         "rsT_v": ertv["vseq"]}
            p3.ev["sync"].append((lambda e: None,
                                  [("sx2", p3.now("sx2"))], [], []))
            barrier(p3)
            with nc.Block() as blk3:
                p3.emit(blk3, semh)

        with ExitStack() as es4:
            sb4 = lambda n, s, d=F32: es4.enter_context(nc.sbuf_tensor(n, s, d))
            bufs4 = (sb4("wgt4", [P, DC, II], BF16), sb4("wut4", [P, DC, II], BF16),
                     sb4("wdt4", [P, IC, D], BF16), sb4("xb4", [P, 2, D], F32),
                     sb4("xqb4", [P, D], BF16), sb4("xqtb4", [P, DC, P], BF16),
                     sb4("hb4", [P, II], F32), sb4("hqb4", [P, II], BF16),
                     sb4("hqtb4", [P, IC, P], BF16), sb4("lnreg4", [P, 11, 176], F32),
                     sb4("stg4", [P, DC, 512], BF16))
            p4 = Prog(CNT)
            ffn_phase(p4, x2h, y_out, ("sy0", "sy1"), bufs4, False, AGW)
            p4.ev["sync"].append((lambda e: None,
                                  [("sy0", p4.now("sy0")), ("sy1", p4.now("sy1"))],
                                  [], []))
            with nc.Block() as blk4:
                p4.emit(blk4, semh)
    return nc


_NC_CACHE = None


def kernel(L0_attn_x, L0_attn_Wq, L0_attn_Wk, L0_attn_Wv,
           L0_ffn_w_gate, L0_ffn_w_up, L0_ffn_w_down):
    global _NC_CACHE
    x = np.ascontiguousarray(np.asarray(L0_attn_x, dtype=np.float32))
    wq = np.ascontiguousarray(np.asarray(L0_attn_Wq, dtype=np.float32))
    wk = np.ascontiguousarray(np.asarray(L0_attn_Wk, dtype=np.float32))
    wv = np.ascontiguousarray(np.asarray(L0_attn_Wv, dtype=np.float32))
    wg = np.ascontiguousarray(np.asarray(L0_ffn_w_gate, dtype=np.float32))
    wu = np.ascontiguousarray(np.asarray(L0_ffn_w_up, dtype=np.float32))
    wd = np.ascontiguousarray(np.asarray(L0_ffn_w_down, dtype=np.float32))
    B, T, _ = x.shape

    acat = np.zeros((NAG_A * P, D), np.float32)
    acat[0:II] = wg
    acat[IC * P:IC * P + II] = wu

    in_maps = []
    for c in range(8):
        b, par = c // 2, c % 2
        xb_seq = x[b].reshape(NKB, 512, D)
        x_own = np.concatenate([xb_seq[g] for g in ASSIGN[par]], axis=0)
        dvals = np.zeros(64, np.float32)
        for j in range(QB):
            for tb in range(2):
                kg = CJ[j] - 2 + tb
                for kc in range(4):
                    dvals[(2 * j + tb) * 4 + kc] = BIG * (
                        512.0 * (ASSIGN[par][j] - kg) - 128.0 * kc)
        dtabv = np.broadcast_to(dvals[None, :], (P, 64)).copy()
        wp = np.zeros((NGRP, P, II), np.float32)
        for i in range(6):
            wp[i, :, 0:D] = acat[(6 * c + i) * P:(6 * c + i + 1) * P]
        wp[6] = wd[c * P:(c + 1) * P]
        in_maps.append({"x_in": np.ascontiguousarray(x_own),
                        "wq": wq, "wk": wk, "wv": wv,
                        "wprep": np.ascontiguousarray(wp),
                        "dtab": np.ascontiguousarray(dtabv)})
    if _NC_CACHE is None:
        _NC_CACHE = build()
    res = run_bass_kernel_spmd(_NC_CACHE, in_maps, core_ids=list(range(8)))
    out = np.empty_like(x)
    for c in range(8):
        b, par = c // 2, c % 2
        yv = np.asarray(res.results[c]["y"]).reshape(QB, 512, D)
        for j, g in enumerate(ASSIGN[par]):
            out[b, g * 512:(g + 1) * 512, :] = yv[j]
    return out


# version 05
# speedup vs baseline: 1.1550x; 1.0150x over previous
import numpy as np
import concourse.bass as bass
import concourse.mybir as mybir
from concourse.bass_utils import run_bass_kernel_spmd

F32 = mybir.dt.float32
BF16 = mybir.dt.bfloat16
I32 = mybir.dt.int32
Alu = mybir.AluOpType
Act = mybir.ActivationFunctionType
AX = mybir.AxisListType.X

P = 128
D = 1024
DC = 8
II = 2816
IC = 22
NT = 4096
NTT = 32
NKB = 16
QB = 8
KM = float(1.5 * 2 ** 23)
EPS = 1e-5
BIG = 3000.0
ASSIGN = [[15, 13, 11, 9, 6, 4, 2, 0],
          [14, 12, 10, 8, 7, 5, 3, 1]]
CJ = [16, 14, 12, 10, 8, 6, 4, 2]
SLOT = [0] * NKB
OFF = [0] * NKB
for _par in (0, 1):
    for _j, _g in enumerate(ASSIGN[_par]):
        SLOT[_g] = _par
        OFF[_g] = _j
GCH = [(k * 512, min(512, II - k * 512)) for k in range(6)]
TRND = [(0, 8), (8, 8), (16, 6)]
NGRP = 7
NAG_A = 48

ENGINES = ("sync", "vector", "scalar", "tensor", "gpsimd")
DEBUG_DUMP = False
PROGS = []


class Prog:

    def __init__(self, cnt):
        self.ev = {e: [] for e in ENGINES}
        self.cnt = cnt
        PROGS.append(self)

    def op(self, eng, fn, incs=(), waits=()):
        w = [(s, v) for (s, v) in waits if v > 0]
        self.ev[eng].append((fn, w, list(incs), []))
        for s, n in incs:
            self.cnt[s] = self.cnt.get(s, 0) + n
        return {s: self.cnt[s] for s, _ in incs}

    def now(self, s):
        return self.cnt.get(s, 0)

    def raw(self, eng, fn, bumps=(), waits=()):
        w = [(s, v) for (s, v) in waits if v > 0]
        self.ev[eng].append((fn, w, [], list(bumps)))
        for s, n in bumps:
            self.cnt[s] = self.cnt.get(s, 0) + n
        return {s: self.cnt[s] for s, _ in bumps}

    def emit(self, block, sems):
        def mk(eng):
            def body(eobj):
                for fn, waits, incs, _bumps in self.ev[eng]:
                    for s, v in waits:
                        eobj.wait_ge(sems[s], v)
                    instr = fn(eobj)
                    for s, n in incs:
                        assert instr is not None, f"no instr for inc {s}"
                        instr.then_inc(sems[s], n)
            return body
        block.sync(mk("sync"))
        block.vector(mk("vector"))
        block.scalar(mk("scalar"))
        block.tensor(mk("tensor"))
        block.gpsimd(mk("gpsimd"))


def build():
    nc = bass.Bass("TRN2", target_bir_lowering=False, debug=False,
                   detect_race_conditions=False)
    dt = nc.dram_tensor
    x_in = dt("x_in", [NT, D], F32, kind="ExternalInput").ap()
    wqi = dt("wq", [D, D], F32, kind="ExternalInput").ap()
    wki = dt("wk", [D, D], F32, kind="ExternalInput").ap()
    wvi = dt("wv", [D, D], F32, kind="ExternalInput").ap()
    wprep = dt("wprep", [NGRP, P, II], F32, kind="ExternalInput").ap()
    dtab = dt("dtab", [P, 64], F32, kind="ExternalInput").ap()
    y_out = dt("y", [NT, D], F32, kind="ExternalOutput").ap()
    KD = "ExternalOutput" if DEBUG_DUMP else "Internal"
    x1h = dt("x1h", [NT, D], F32, kind=KD).ap()
    x2h = dt("x2h", [NT, D], F32, kind=KD).ap()
    xaownh = dt("xaownh", [QB, D, 512], BF16, kind=KD).ap()
    srcb = dt("srcb", [QB, D, 512], BF16, kind="Internal").ap()
    bbt = dt("bbt", [QB, 2, D, 512], BF16, kind=KD).ap()
    kth = dt("kth", [NKB, D, 512], BF16, kind=KD).ap()
    vth = dt("vth", [NKB, 512, D], BF16, kind=KD).ap()
    srcA = dt("srcA", [6, D, P], BF16, kind="Internal").ap()
    outA = dt("outA", [NAG_A, D, P], BF16, kind="Internal").ap()
    srcBd = dt("srcBd", [1, II, P], BF16, kind="Internal").ap()
    outB = dt("outB", [8, II, P], BF16, kind="Internal").ap()

    from contextlib import ExitStack
    with ExitStack() as es:
        sbuf = lambda n, s, d=F32: es.enter_context(nc.sbuf_tensor(n, s, d))
        psA = es.enter_context(nc.psum_tensor("psA", [P, 4096], F32))
        ident = sbuf("ident", [P, P], BF16)
        dtb = sbuf("dtb", [P, 64], F32)
        ones = sbuf("ones", [P, 1], BF16)
        qcb = sbuf("qcb", [P, II], F32)
        qqn = sbuf("qqn", [P, II], F32)
        qbm = sbuf("qbm", [P, II // 16], F32)
        qsp = sbuf("qsp", [P, II // 16], F32)
        qrs = sbuf("qrs", [P, II // 16], F32)
        LNW = 176

        def ln_views(lnreg):
            R = {}
            for i, n in enumerate(("s1v", "s2v", "mv", "m2v", "vev", "r0v",
                                   "rsd0", "rsda", "rsdb", "rstdv", "nmrv")):
                R[n] = lnreg[:, i, :]
            R["s1c"] = lnreg[:, 0, 0:1]
            R["s2c"] = lnreg[:, 1, 0:1]
            R["rstdv1"] = lnreg[:, 9, 0:1]
            R["nmrv1"] = lnreg[:, 10, 0:1]
            return R

        SEMS = (["sx0", "sx1", "st0", "st1", "ssrc", "swt", "sy0", "sy1",
                 "sseq", "vseq", "tseq", "gseq", "agc", "swg", "swst",
                 "skv0", "skv1", "skt0", "skt1", "svt0", "svt1", "sxa",
                 "sx1l", "sx2", "gwq", "skst", "bar"])
        semh = {n: es.enter_context(nc.semaphore(n)) for n in SEMS}
        CNT = {n: 0 for n in SEMS}

        ps_gate = [psA[:, 0:512], psA[:, 512:1024]]
        ps_up = [psA[:, 1024:1536], psA[:, 1536:2048]]
        ps_down = [psA[:, 2048:2560], psA[:, 2560:3072]]
        ps_tp = psA[:, 3072:3584].bitcast(BF16)
        ps_tp2 = psA[:, 3584:4096].bitcast(BF16)
        ps_k = [psA[:, 0:512], psA[:, 512:1024]]
        ps_v = [[psA[:, 1024:1536], psA[:, 1536:2048]],
                [psA[:, 2048:2560], psA[:, 2560:3072]]]
        ps_s = [psA[:, 0:512], psA[:, 512:1024]]
        ps_rsv = psA[:, 1024:1536]
        ps_rt = psA[:, 1536:1540]
        ps_q = [psA[:, 0:512], psA[:, 512:1024]]
        ps_o = [[psA[:, (2 * qc + dvh) * 512:(2 * qc + dvh + 1) * 512]
                 for dvh in range(2)] for qc in range(4)]

        def q_pipeline(p, xn_buf, src_ap, dst_ap, w, extra_w_v1=()):
            nb = w // 16
            nbs = II // 16
            xnb = xn_buf[:, 0:w]

            def v1(v):
                v.tensor_reduce(qbm[:, :nbs],
                                xn_buf[:, 0:II].rearrange("p (b k) -> p b k", k=16),
                                AX, Alu.max, apply_absolute_value=True)
                v.tensor_scalar(qsp[:, :nbs], qbm[:, :nbs], 1.0 / 6.0, 1e-12,
                                Alu.mult, Alu.max)
                v.reciprocal(qrs[:, :nbs], qsp[:, :nbs])
                v.tensor_tensor(xnb.rearrange("p (b k) -> p b k", k=16),
                                src_ap.rearrange("p (b k) -> p b k", k=16),
                                qrs[:, :nb].unsqueeze(2).broadcast_to([P, nb, 16]),
                                Alu.mult)
                v.tensor_scalar(qqn[:, :w].bitcast(I32), xnb.bitcast(I32),
                                0x7FFFFFFF, 21, Alu.bitwise_and,
                                Alu.logical_shift_right)
                v.tensor_scalar(qqn[:, :w].bitcast(I32), qqn[:, :w].bitcast(I32),
                                1, 508, Alu.add, Alu.max)
                v.tensor_scalar(qqn[:, :w].bitcast(I32), qqn[:, :w].bitcast(I32),
                                1, 22, Alu.logical_shift_right,
                                Alu.logical_shift_left)
                v.tensor_scalar(qcb[:, :w], xnb, -1.0, 1.0, Alu.max, Alu.min)
                v.tensor_scalar(qcb[:, :w], qcb[:, :w], 2.0, KM,
                                Alu.mult, Alu.add)
                v.tensor_scalar(qcb[:, :w], qcb[:, :w], 0.5, -KM / 2,
                                Alu.mult, Alu.add)
                v.tensor_tensor(qcb[:, :w], qcb[:, :w], qqn[:, :w], Alu.mult)
                return v.tensor_tensor(dst_ap.rearrange("p (b k) -> p b k", k=16),
                                       qcb[:, :w].rearrange("p (b k) -> p b k", k=16),
                                       qsp[:, :nb].unsqueeze(2).broadcast_to(
                                           [P, nb, 16]), Alu.mult)
            e4 = p.op("vector", v1, incs=[("vseq", 1)], waits=list(extra_w_v1))
            return e4, e4

        def ln_pipeline(p, sq_waits, x_ap, n, R):
            def s_sq(s):
                s.activation(qcb[:, 0:n], x_ap, Act.Copy, accum_out=R["s1c"])
                return s.activation(qcb[:, 0:n], x_ap, Act.Square,
                                    accum_out=R["s2c"])
            e1 = p.op("scalar", s_sq, incs=[("sseq", 1)], waits=list(sq_waits))

            def v_st(v):
                v.tensor_scalar(R["mv"], R["s1v"], 1.0 / n, None, Alu.mult)
                v.tensor_tensor(R["m2v"], R["mv"], R["mv"], Alu.mult)
                v.tensor_scalar(R["vev"], R["s2v"], 1.0 / n, EPS,
                                Alu.mult, Alu.add)
                v.tensor_tensor(R["vev"], R["vev"], R["m2v"], Alu.subtract)
                v.tensor_scalar(R["r0v"], R["vev"], 0.5, None, Alu.mult)
                v.tensor_scalar(R["rsd0"].bitcast(I32), R["vev"].bitcast(I32),
                                1, None, Alu.logical_shift_right)
                v.tensor_scalar(R["rsd0"].bitcast(I32), R["rsd0"].bitcast(I32),
                                0xFFFFFFFF, None, Alu.bitwise_xor)
                v.tensor_scalar(R["rsd0"].bitcast(I32), R["rsd0"].bitcast(I32),
                                0x5f3759df + 1, None, Alu.add)
                for it in range(3):
                    last = it == 2
                    v.tensor_tensor(R["rsda"], R["rsd0"], R["rsd0"], Alu.mult)
                    v.tensor_tensor(R["rsda"], R["r0v"], R["rsda"], Alu.mult)
                    v.tensor_scalar(R["rsda"], R["rsda"], -1.0, 1.5,
                                    Alu.mult, Alu.add)
                    v.tensor_tensor(R["rstdv"] if last else R["rsd0"],
                                    R["rsd0"], R["rsda"], Alu.mult)
                return v.scalar_tensor_tensor(R["nmrv"], R["mv"], -1.0, R["rstdv"],
                                              Alu.mult, Alu.mult)
            return p.op("vector", v_st, incs=[("vseq", 1)],
                        waits=[("sseq", e1["sseq"])])

        def barrier(p):
            v = p.now("bar") + len(ENGINES)
            for e in ENGINES:
                p.op(e, lambda eng: eng.nop(), incs=[("bar", 1)])
            for e in ENGINES:
                p.ev[e].append((lambda eng: None, [("bar", v)], [], []))

        with ExitStack() as es0:
            sb0 = lambda n, s, d=F32: es0.enter_context(nc.sbuf_tensor(n, s, d))
            wgb = sb0("wgb", [P, II], F32)
            wqg = sb0("wqg", [P, II], BF16)
            wstg = sb0("wstg", [P, IC, P], BF16)
            iof = sb0("iof", [P, P], F32)
            p0 = Prog(CNT)
            p0.op("sync", lambda e: e.dma_start(out=dtb[:], in_=dtab[:]),
                  incs=[("swg", 16)])

            def g_iota(g):
                return g.iota(iof[:], [[1, P]], base=0, channel_multiplier=-1,
                              allow_small_or_imprecise_dtypes=True)
            eio = p0.op("gpsimd", g_iota, incs=[("gseq", 1)])

            def v_id(v):
                v.memset(ones[:], 1.0)
                return v.tensor_scalar(ident[:], iof[:], 0.0, None, Alu.is_equal)
            p0.op("vector", v_id, incs=[("vseq", 1)], waits=[("gseq", eio["gseq"])])

            for i in range(NGRP):
                el = p0.op("sync",
                           lambda e, i=i: e.dma_start(out=wgb[:], in_=wprep[i]),
                           incs=[("swg", 16)], waits=[("vseq", p0.now("vseq"))])
                eq, _ = q_pipeline(p0, wgb, wgb[:, :], wqg[:, :], II,
                                   extra_w_v1=[("swg", el["swg"])])
                last_evac = None
                for (i0, ntp) in TRND:
                    def t_tp(t, i0=i0, ntp=ntp):
                        r = None
                        for k in range(ntp):
                            r = t.transpose(ps_tp[:, k * P:(k + 1) * P],
                                            wqg[:, (i0 + k) * P:(i0 + k + 1) * P],
                                            ident[:])
                        return r
                    wts = [("vseq", eq["vseq"])]
                    if last_evac is not None:
                        wts.append(("vseq", last_evac["vseq"]))
                    ett = p0.op("tensor", t_tp, incs=[("tseq", 1)], waits=wts)

                    def v_ev(v, i0=i0, ntp=ntp):
                        return v.tensor_copy(
                            wstg[:, i0:i0 + ntp, :],
                            ps_tp[:, 0:ntp * P].rearrange("p (c q) -> p c q", q=P))
                    last_evac = p0.op("vector", v_ev, incs=[("vseq", 1)],
                                      waits=[("tseq", ett["tseq"])])
                if i < 6:
                    def st(e, i=i):
                        return e.dma_start(
                            out=srcA[i].rearrange("(c p) q -> p c q", p=P),
                            in_=wstg[:, 0:DC, :])
                else:
                    def st(e):
                        return e.dma_start(
                            out=srcBd[0].rearrange("(c p) q -> p c q", p=P),
                            in_=wstg[:, :, :])
                p0.op("sync", st, incs=[("swst", 16)],
                      waits=[("vseq", last_evac["vseq"])])

            def ag_a(g):
                return g.collective_compute(
                    "AllGather", Alu.bypass, replica_groups=[list(range(8))],
                    ins=[srcA[:].rearrange("g d p -> (g d) p")],
                    outs=[outA[:].rearrange("g d p -> (g d) p")])
            ea = p0.op("gpsimd", ag_a, incs=[("agc", 1)],
                       waits=[("swst", p0.now("swst"))])
            p0.ev["gpsimd"].append((lambda g: None, [("agc", ea["agc"])], [], []))

            def ag_b(g):
                return g.collective_compute(
                    "AllGather", Alu.bypass, replica_groups=[list(range(8))],
                    ins=[srcBd[:].rearrange("g d p -> (g d) p")],
                    outs=[outB[:].rearrange("g d p -> (g d) p")])
            eb = p0.op("gpsimd", ag_b, incs=[("agc", 1)])
            p0.ev["gpsimd"].append((lambda g: None, [("agc", eb["agc"])], [], []))
            AGW = eb["agc"]

            with nc.Block() as blk0:
                p0.emit(blk0, semh)

        def ffn_phase(p, src_hbm, dst_hbm, st_sems, bufs, with_ln2, agw):
            (wgt, wut, wdt, xb, xqb, xqtb, hb, hqb, hqtb, lnreg, stg) = bufs
            R = ln_views(lnreg)
            silub = [qcb[:, 0:512], qcb[:, 512:1024]]

            def v_init(v):
                v.memset(hb[:, :], 1.0)
                return v.memset(lnreg[:].rearrange("p a b -> p (a b)"), 1.0)
            p.op("vector", v_init, incs=[("vseq", 1)])

            swth = semh["swt"]

            def wload(e):
                for g in range(IC):
                    e.dma_start(out=wgt[:, :, g * P:(g + 1) * P],
                                in_=outA[g].rearrange("(c p) q -> p c q", p=P)
                                ).then_inc(swth, 16)
                    e.dma_start(out=wut[:, :, g * P:(g + 1) * P],
                                in_=outA[IC + g].rearrange("(c p) q -> p c q", p=P)
                                ).then_inc(swth, 16)
                for g in range(DC):
                    e.dma_start(out=wdt[:, :, g * P:(g + 1) * P],
                                in_=outB[g].rearrange("(c p) q -> p c q", p=P)
                                ).then_inc(swth, 16)
                return None
            ewl = p.raw("sync", wload, bumps=[("swt", 16 * (2 * IC + DC))],
                        waits=[("agc", agw)])

            prev = {}
            for tt in range(NTT):
                par = tt % 2
                sxn = "sx0" if par == 0 else "sx1"
                stn = st_sems[par]
                xtile = xb[:, par, :]
                lw = []
                if tt >= 2:
                    q = prev[tt - 2]
                    lw = [(stn, q["store"]), ("sseq", q["lastread_s"]),
                          ("vseq", q["lastread_v"])]
                ex = p.op("sync",
                          lambda e, tt=tt, xtile=xtile: e.dma_start(
                              out=xtile, in_=src_hbm[tt * P:(tt + 1) * P, :]),
                          incs=[(sxn, 16)], waits=lw)
                sq_w = [(sxn, ex[sxn])]
                if tt >= 1:
                    sq_w += [("vseq", prev[tt - 1]["ln2stats_v"]),
                             ("vseq", prev[tt - 1]["qcb_free_g"])]
                eln = ln_pipeline(p, sq_w, xtile, D, R)

                def s_norm(s, xtile=xtile):
                    return s.activation(hb[:, 0:D], xtile, Act.Identity,
                                        bias=R["nmrv1"], scale=R["rstdv1"])
                en = p.op("scalar", s_norm, incs=[("sseq", 1)],
                          waits=[("vseq", eln["vseq"])])
                eqa, _ = q_pipeline(p, hb, hb[:, 0:D], xqb[:, :], D,
                                    extra_w_v1=[("sseq", en["sseq"])])

                def t_tpq(t):
                    r = None
                    for k in range(DC):
                        r = t.transpose(ps_tp[:, k * P:(k + 1) * P],
                                        xqb[:, k * P:(k + 1) * P], ident[:])
                    return r
                etq = p.op("tensor", t_tpq, incs=[("tseq", 1)],
                           waits=[("vseq", eqa["vseq"])])

                def v_evq(v):
                    return v.tensor_copy(xqtb[:],
                                         ps_tp[:].rearrange("p (c q) -> p c q", q=P))
                eevq = p.op("vector", v_evq, incs=[("vseq", 1)],
                            waits=[("tseq", etq["tseq"])])

                hchunk_v = []
                for k, (lo, w) in enumerate(GCH):
                    wts = [("vseq", eevq["vseq"]), ("swt", ewl["swt"])]
                    if k >= 2:
                        wts.append(("vseq", hchunk_v[k - 2]["vseq"]))
                    elif tt >= 1:
                        wts.append(("vseq", prev[tt - 1]["hch_v"][4 + k]))

                    def t_gu(t, lo=lo, w=w, k=k):
                        r = None
                        for dc in range(DC):
                            t.matmul(ps_gate[k % 2][:, 0:w], xqtb[:, dc, :],
                                     wgt[:, dc, lo:lo + w],
                                     start=(dc == 0), stop=(dc == DC - 1))
                            r = t.matmul(ps_up[k % 2][:, 0:w], xqtb[:, dc, :],
                                         wut[:, dc, lo:lo + w],
                                         start=(dc == 0), stop=(dc == DC - 1))
                        return r
                    etg = p.op("tensor", t_gu, incs=[("tseq", 1)], waits=wts)

                    def s_si(s, w=w, k=k):
                        return s.activation(silub[k % 2][:, 0:w],
                                            ps_gate[k % 2][:, 0:w], Act.Silu)
                    esi = p.op("scalar", s_si, incs=[("sseq", 1)],
                               waits=[("tseq", etg["tseq"])])

                    def v_h(v, lo=lo, w=w, k=k):
                        return v.tensor_tensor(hb[:, lo:lo + w], ps_up[k % 2][:, 0:w],
                                               silub[k % 2][:, 0:w], Alu.mult)
                    ehm = p.op("vector", v_h, incs=[("vseq", 1)],
                               waits=[("sseq", esi["sseq"])])
                    hchunk_v.append(ehm)
                hch_vals = [e["vseq"] for e in hchunk_v]

                HSP = 1408
                HCH = HSP // P

                def q_elem(v, lo, w):
                    xnb = hb[:, lo:lo + w]
                    nb0, nb1 = lo // 16, (lo + w) // 16
                    nbw = nb1 - nb0
                    v.tensor_tensor(xnb.rearrange("p (b k) -> p b k", k=16),
                                    hb[:, lo:lo + w].rearrange(
                                        "p (b k) -> p b k", k=16),
                                    qrs[:, nb0:nb1].unsqueeze(2).broadcast_to(
                                        [P, nbw, 16]), Alu.mult)
                    v.tensor_scalar(qqn[:, lo:lo + w].bitcast(I32),
                                    xnb.bitcast(I32), 0x7FFFFFFF, 21,
                                    Alu.bitwise_and, Alu.logical_shift_right)
                    v.tensor_scalar(qqn[:, lo:lo + w].bitcast(I32),
                                    qqn[:, lo:lo + w].bitcast(I32),
                                    1, 508, Alu.add, Alu.max)
                    v.tensor_scalar(qqn[:, lo:lo + w].bitcast(I32),
                                    qqn[:, lo:lo + w].bitcast(I32),
                                    1, 22, Alu.logical_shift_right,
                                    Alu.logical_shift_left)
                    v.tensor_scalar(qcb[:, lo:lo + w], xnb, -1.0, 1.0,
                                    Alu.max, Alu.min)
                    v.tensor_scalar(qcb[:, lo:lo + w], qcb[:, lo:lo + w],
                                    2.0, KM, Alu.mult, Alu.add)
                    v.tensor_scalar(qcb[:, lo:lo + w], qcb[:, lo:lo + w],
                                    0.5, -KM / 2, Alu.mult, Alu.add)
                    v.tensor_tensor(qcb[:, lo:lo + w], qcb[:, lo:lo + w],
                                    qqn[:, lo:lo + w], Alu.mult)
                    return v.tensor_tensor(
                        hqb[:, lo:lo + w].rearrange("p (b k) -> p b k", k=16),
                        qcb[:, lo:lo + w].rearrange("p (b k) -> p b k", k=16),
                        qsp[:, nb0:nb1].unsqueeze(2).broadcast_to([P, nbw, 16]),
                        Alu.mult)

                def v_hq1(v):
                    v.tensor_reduce(qbm[:],
                                    hb[:, 0:II].rearrange("p (b k) -> p b k", k=16),
                                    AX, Alu.max, apply_absolute_value=True)
                    v.tensor_scalar(qsp[:], qbm[:], 1.0 / 6.0, 1e-12,
                                    Alu.mult, Alu.max)
                    v.reciprocal(qrs[:], qsp[:])
                    return q_elem(v, 0, HSP)
                eqh1 = p.op("vector", v_hq1, incs=[("vseq", 1)])
                eqh2 = p.op("vector", lambda v: q_elem(v, HSP, II - HSP),
                            incs=[("vseq", 1)])
                egh = eqh2

                evac_s = None
                evac_half = [None, None]
                for (i0, ntp, half) in ((0, 8, 0), (8, 3, 0),
                                        (11, 8, 1), (19, 3, 1)):
                    wts = [("vseq", (eqh1 if half == 0 else eqh2)["vseq"])]
                    if evac_s is not None:
                        wts.append(("sseq", evac_s["sseq"]))
                    elif tt >= 1:
                        wts.append(("sseq", prev[tt - 1]["hqev_s"]))

                    def t_th(t, i0=i0, ntp=ntp):
                        r = None
                        for k in range(ntp):
                            r = t.transpose(ps_tp2[:, k * P:(k + 1) * P],
                                            hqb[:, (i0 + k) * P:(i0 + k + 1) * P],
                                            ident[:])
                        return r
                    eth = p.op("tensor", t_th, incs=[("tseq", 1)], waits=wts)

                    def s_ev(s, i0=i0, ntp=ntp):
                        return s.copy(hqtb[:, i0:i0 + ntp, :],
                                      ps_tp2[:, 0:ntp * P].rearrange(
                                          "p (c q) -> p c q", q=P))
                    evac_s = p.op("scalar", s_ev, incs=[("sseq", 1)],
                                  waits=[("tseq", eth["tseq"])])
                    evac_half[half] = evac_s
                    if i0 == 8:
                        dwts = [("sseq", evac_s["sseq"])]
                        if tt >= 1:
                            dwts.append(("vseq", prev[tt - 1]["x1_v"]))

                        def t_dn_a(t):
                            r = None
                            for ic in range(HCH):
                                t.matmul(ps_down[0], hqtb[:, ic, :],
                                         wdt[:, ic, 0:512],
                                         start=(ic == 0), stop=False)
                                r = t.matmul(ps_down[1], hqtb[:, ic, :],
                                             wdt[:, ic, 512:D],
                                             start=(ic == 0), stop=False)
                            return r
                        p.op("tensor", t_dn_a, incs=[("tseq", 1)], waits=dwts)

                def t_dn_b0(t):
                    r = None
                    for ic in range(HCH, IC):
                        r = t.matmul(ps_down[0], hqtb[:, ic, :], wdt[:, ic, 0:512],
                                     start=False, stop=(ic == IC - 1))
                    return r
                edn0 = p.op("tensor", t_dn_b0, incs=[("tseq", 1)],
                            waits=[("sseq", evac_half[1]["sseq"])])

                def v_x1a(v, xtile=xtile):
                    return v.scalar_tensor_tensor(xtile[:, 0:512], ps_down[0], 0.5,
                                                  xtile[:, 0:512], Alu.mult, Alu.add)
                p.op("vector", v_x1a, incs=[("vseq", 1)],
                     waits=[("tseq", edn0["tseq"])])

                def t_dn_b1(t):
                    r = None
                    for ic in range(HCH, IC):
                        r = t.matmul(ps_down[1], hqtb[:, ic, :], wdt[:, ic, 512:D],
                                     start=False, stop=(ic == IC - 1))
                    return r
                edn = p.op("tensor", t_dn_b1, incs=[("tseq", 1)])

                def v_x1b(v, xtile=xtile):
                    return v.scalar_tensor_tensor(xtile[:, 512:D], ps_down[1], 0.5,
                                                  xtile[:, 512:D], Alu.mult, Alu.add)
                ex1 = p.op("vector", v_x1b, incs=[("vseq", 1)],
                           waits=[("tseq", edn["tseq"])])
                est = p.op("sync",
                           lambda e, tt=tt, xtile=xtile: e.dma_start(
                               out=dst_hbm[tt * P:(tt + 1) * P, :], in_=xtile),
                           incs=[(stn, 16)], waits=[("vseq", ex1["vseq"])])
                rec = {"store": est[stn], "x1_v": ex1["vseq"], "hch_v": hch_vals,
                       "hqev_s": evac_s["sseq"], "qcb_free_g": egh["vseq"]}

                if with_ln2:
                    eln2 = ln_pipeline(p, [("vseq", ex1["vseq"]),
                                           ("vseq", egh["vseq"])], xtile, D, R)

                    def s_xa(s, xtile=xtile):
                        return s.activation(xqb[:], xtile, Act.Identity,
                                            bias=R["nmrv1"], scale=R["rstdv1"])
                    exa = p.op("scalar", s_xa, incs=[("sseq", 1)],
                               waits=[("vseq", eln2["vseq"])])

                    def t_ta(t):
                        r = None
                        for k in range(DC):
                            r = t.transpose(ps_tp[:, k * P:(k + 1) * P],
                                            xqb[:, k * P:(k + 1) * P], ident[:])
                        return r
                    eta = p.op("tensor", t_ta, incs=[("tseq", 1)],
                               waits=[("sseq", exa["sseq"])])
                    rnd, sl = tt // 4, tt % 4

                    def v_ea(v, sl=sl):
                        return v.tensor_copy(
                            stg[:, :, sl * P:(sl + 1) * P],
                            ps_tp[:].rearrange("p (c q) -> p c q", q=P))
                    wts = [("tseq", eta["tseq"])]
                    if sl == 0 and rnd >= 1:
                        wts.append(("ssrc", 32 * rnd))
                    eea = p.op("vector", v_ea, incs=[("vseq", 1)], waits=wts)
                    rec["ln2stats_v"] = eln2["vseq"]
                    if sl == 3:
                        ssrch = semh["ssrc"]

                        def st_src(e, rnd=rnd):
                            e.dma_start(
                                out=srcb[rnd].rearrange("(c p) t -> p c t", p=P),
                                in_=stg[:]).then_inc(ssrch, 16)
                            e.dma_start(
                                out=xaownh[rnd].rearrange("(c p) t -> p c t", p=P),
                                in_=stg[:]).then_inc(ssrch, 16)
                            return None
                        esrc = p.raw("sync", st_src, bumps=[("ssrc", 32)],
                                     waits=[("vseq", eea["vseq"])])

                        def g_ag(g, rnd=rnd):
                            return g.collective_compute(
                                "AllGather", Alu.bypass,
                                replica_groups=[[0, 1], [2, 3], [4, 5], [6, 7]],
                                ins=[srcb[rnd]],
                                outs=[bbt[rnd].rearrange("s d t -> (s d) t")])
                        eag = p.op("gpsimd", g_ag, incs=[("agc", 1)],
                                   waits=[("ssrc", esrc["ssrc"])])
                        p.ev["gpsimd"].append(
                            (lambda g: None, [("agc", eag["agc"])], [], []))
                else:
                    rec["ln2stats_v"] = ex1["vseq"]
                rec["lastread_s"] = p.now("sseq")
                rec["lastread_v"] = p.now("vseq")
                prev[tt] = rec
            p.ev["sync"].append((lambda e: None,
                                 [(st_sems[0], p.now(st_sems[0])),
                                  (st_sems[1], p.now(st_sems[1]))], [], []))

        with ExitStack() as es1:
            sb1 = lambda n, s, d=F32: es1.enter_context(nc.sbuf_tensor(n, s, d))
            bufs1 = (sb1("wgt", [P, DC, II], BF16), sb1("wut", [P, DC, II], BF16),
                     sb1("wdt", [P, IC, D], BF16), sb1("xb", [P, 2, D], F32),
                     sb1("xqb", [P, D], BF16), sb1("xqtb", [P, DC, P], BF16),
                     sb1("hb", [P, II], F32), sb1("hqb", [P, II], BF16),
                     sb1("hqtb", [P, IC, P], BF16), sb1("lnreg1", [P, 11, 176], F32),
                     sb1("stg", [P, DC, 512], BF16))
            p1 = Prog(CNT)
            ffn_phase(p1, x_in, x1h, ("st0", "st1"), bufs1, True, AGW)
            AGXA = p1.now("agc")
            barrier(p1)
            with nc.Block() as blk1:
                p1.emit(blk1, semh)

        with ExitStack() as es3:
            sb3 = lambda n, s, d=F32: es3.enter_context(nc.sbuf_tensor(n, s, d))
            wq16 = sb3("wq16", [P, DC, D], BF16)
            wk16 = sb3("wk16", [P, DC, D], BF16)
            wv16 = sb3("wv16", [P, DC, D], BF16)
            qtb = sb3("qtb", [P, DC, 512], BF16)
            xaq = sb3("xaq", [P, DC, 512], BF16)
            ktb = sb3("ktb", [P, 2, DC, 512], BF16)
            vb = sb3("vb", [P, 2, 4, D], BF16)
            ptb = sb3("ptb", [P, 64, 512], BF16)
            x1j = sb3("x1j", [P, 4, D], F32)
            d3 = sb3("d3", [P, 512], F32)
            mskb = sb3("mskb", [P, 512], F32)
            rsTin = mskb[:, 0:LNW]
            rsTw = mskb[:, 256:256 + LNW]
            rse = sb3("rse", [1, 512], F32)
            onesf = sb3("onesf", [1, 1], F32)
            p3 = Prog(CNT)

            def g_d3(g):
                return g.iota(d3[:], [[int(BIG), 512]], base=0,
                              channel_multiplier=-int(BIG),
                              allow_small_or_imprecise_dtypes=True)
            p3.op("gpsimd", g_d3, incs=[("gseq", 1)])
            p3.op("vector", lambda v: v.memset(onesf[:], 1.0), incs=[("vseq", 1)])

            gwqh = semh["gwq"]

            def g_cast(g):
                g.dma_start(out=wq16[:],
                            in_=wqi[:].rearrange("(c p) d -> p c d", p=P)
                            ).then_inc(gwqh, 16)
                g.dma_start(out=wk16[:],
                            in_=wki[:].rearrange("(c p) d -> p c d", p=P)
                            ).then_inc(gwqh, 16)
                g.dma_start(out=wv16[:],
                            in_=wvi[:].rearrange("(c p) d -> p c d", p=P)
                            ).then_inc(gwqh, 16)
                return None
            egc = p3.raw("gpsimd", g_cast, bumps=[("gwq", 48)])

            kst = {}
            for e2 in range(NKB):
                kg = NKB - 1 - e2
                par = e2 % 2
                skvn = "skv0" if par == 0 else "skv1"
                lwts = [("agc", AGXA - QB + OFF[kg] + 1)]
                if e2 >= 2:
                    lwts.append(("tseq", kst[e2 - 2]["t_end"]))

                def l_kv(e, kg=kg, par=par):
                    return e.dma_start(
                        out=ktb[:, par, :, :],
                        in_=bbt[OFF[kg], SLOT[kg]].rearrange("(c p) t -> p c t", p=P))
                ekl = p3.op("sync", l_kv, incs=[(skvn, 16)], waits=lwts)

                ks_evts = []
                for dkc in range(DC):
                    wts = [(skvn, ekl[skvn]), ("gwq", egc["gwq"])]
                    if dkc >= 2:
                        wts.append(("sseq", ks_evts[dkc - 2]["sseq"]))
                    elif e2 >= 1:
                        wts.append(("sseq", kst[e2 - 1]["kev_s"]))

                    def t_k(t, dkc=dkc, par=par):
                        r = None
                        for dc in range(DC):
                            r = t.matmul(ps_k[dkc % 2],
                                         wk16[:, dc, dkc * P:(dkc + 1) * P],
                                         ktb[:, par, dc, :],
                                         start=(dc == 0), stop=(dc == DC - 1))
                        return r
                    etk = p3.op("tensor", t_k, incs=[("tseq", 1)], waits=wts)
                    sw = [("tseq", etk["tseq"])]
                    if e2 >= 1 and dkc == 0:
                        sw.append(("skst", kst[e2 - 1]["kst"]))

                    def s_ke(s, dkc=dkc):
                        return s.copy(ptb[:, dkc, :], ps_k[dkc % 2])
                    eke = p3.op("scalar", s_ke, incs=[("sseq", 1)], waits=sw)
                    ks_evts.append(eke)
                ekst = p3.op("sync",
                             lambda e, kg=kg: e.dma_start(
                                 out=kth[kg].rearrange("(c p) t -> p c t", p=P),
                                 in_=ptb[:, 0:DC, :]),
                             incs=[("skst", 16)],
                             waits=[("sseq", ks_evts[-1]["sseq"])])

                vs_evts = []
                for tc in range(4):
                    wts = []
                    if tc >= 2:
                        wts.append(("vseq", vs_evts[tc - 2]["vseq"]))
                    elif e2 >= 1:
                        wts.append(("vseq", kst[e2 - 1]["vev_v"]))

                    def t_v(t, tc=tc, par=par):
                        r = None
                        for dc in range(DC):
                            t.matmul(ps_v[tc % 2][0],
                                     ktb[:, par, dc, tc * P:(tc + 1) * P],
                                     wv16[:, dc, 0:512],
                                     start=(dc == 0), stop=(dc == DC - 1))
                            r = t.matmul(ps_v[tc % 2][1],
                                         ktb[:, par, dc, tc * P:(tc + 1) * P],
                                         wv16[:, dc, 512:D],
                                         start=(dc == 0), stop=(dc == DC - 1))
                        return r
                    etv = p3.op("tensor", t_v, incs=[("tseq", 1)], waits=wts)
                    vw = [("tseq", etv["tseq"])]
                    if e2 >= 1 and tc == 0:
                        vw.append(("skst", kst[e2 - 1]["vst"]))

                    def v_ve(v, tc=tc):
                        v.tensor_copy(ptb[:, 8 + 2 * tc, :], ps_v[tc % 2][0])
                        return v.tensor_copy(ptb[:, 9 + 2 * tc, :], ps_v[tc % 2][1])
                    eve = p3.op("vector", v_ve, incs=[("vseq", 1)], waits=vw)
                    vs_evts.append(eve)
                evst = p3.op("sync",
                             lambda e, kg=kg: e.dma_start(
                                 out=vth[kg].rearrange("(tc p) d -> p tc d", p=P),
                                 in_=ptb[:, 8:16, :].rearrange(
                                     "p (tc h) t -> p tc (h t)", h=2)),
                             incs=[("skst", 16)],
                             waits=[("vseq", vs_evts[-1]["vseq"])])
                kst[e2] = {"t_end": p3.now("tseq"), "kev_s": ks_evts[-1]["sseq"],
                           "vev_v": vs_evts[-1]["vseq"], "kst": ekst["skst"],
                           "vst": evst["skst"]}
            P3A_SKST = p3.now("skst")
            P3A_TSEQ = p3.now("tseq")

            prevj = None
            for j in range(QB):
                cj = CJ[j]
                lw = [("skst", P3A_SKST)]
                if j >= 1:
                    lw.append(("tseq", prevj["q_end"]))
                exq = p3.op("sync",
                            lambda e, j=j: e.dma_start(
                                out=xaq[:],
                                in_=xaownh[j].rearrange("(c p) t -> p c t", p=P)),
                            incs=[("sxa", 16)], waits=lw)
                lw2 = [("sx2", prevj["x2st"])] if j >= 1 else []
                ex1l = p3.op("sync",
                             lambda e, j=j: e.dma_start(
                                 out=x1j[:],
                                 in_=x1h[j * 512:(j + 1) * 512, :].rearrange(
                                     "(q p) d -> p q d", p=P)),
                             incs=[("sx1l", 16)], waits=lw2)

                q_evts = []
                for dkc in range(DC):
                    wts = [("sxa", exq["sxa"]), ("gwq", egc["gwq"])]
                    if dkc >= 2:
                        wts.append(("sseq", q_evts[dkc - 2]["sseq"]))
                    if dkc == 0 and j >= 1:
                        wts.append(("vseq", prevj["x2_v"]))

                    def t_q(t, dkc=dkc):
                        r = None
                        for dc in range(DC):
                            r = t.matmul(ps_q[dkc % 2],
                                         wq16[:, dc, dkc * P:(dkc + 1) * P],
                                         xaq[:, dc, :],
                                         start=(dc == 0), stop=(dc == DC - 1))
                        return r
                    etq2 = p3.op("tensor", t_q, incs=[("tseq", 1)], waits=wts)

                    def s_qe(s, dkc=dkc):
                        return s.copy(qtb[:, dkc, :], ps_q[dkc % 2])
                    eqe = p3.op("scalar", s_qe, incs=[("sseq", 1)],
                                waits=[("tseq", etq2["tseq"])])
                    q_evts.append(eqe)
                q_end = p3.now("tseq")

                smm = {}
                expv = {}
                ktload = {}
                for kg in range(cj):
                    sktn = "skt0" if kg % 2 == 0 else "skt1"
                    lkw = [("tseq", P3A_TSEQ if j == 0 and kg < 2 else 0)]
                    if kg >= 2:
                        lkw.append(("tseq", smm[(kg - 2, 3)]))
                    elif j >= 1 and prevj["nkt"] >= 2 - kg:
                        lkw.append(("tseq", prevj["smm_last"][kg - 2]))
                    ekt = p3.op("sync",
                                lambda e, kg=kg: e.dma_start(
                                    out=ktb[:, kg % 2, :, :],
                                    in_=kth[kg].rearrange("(c p) t -> p c t", p=P)),
                                incs=[(sktn, 16)], waits=lkw)
                    ktload[kg] = ekt[sktn]
                    for kc in range(4):
                        slot = kg * 4 + kc
                        wts = [(sktn, ektl) for ektl in [ktload[kg]]]
                        wts.append(("sseq", q_evts[-1]["sseq"]))
                        if slot >= 2:
                            wts.append(("sseq", expv[slot - 2]))
                        elif j >= 1:
                            wts.append(("sseq", prevj["exp_last"][slot - 2]))
                        if slot == 0:
                            wts.append(("vseq", prevj["rsT_v"] if j >= 1 else 0))

                        def t_s(t, kg=kg, kc=kc):
                            r = None
                            for dc in range(DC):
                                r = t.matmul(ps_s[kc % 2],
                                             ktb[:, kg % 2, dc, kc * P:(kc + 1) * P],
                                             qtb[:, dc, :],
                                             start=(dc == 0), stop=(dc == DC - 1))
                            return r
                        ets = p3.op("tensor", t_s, incs=[("tseq", 1)], waits=wts)
                        smm[(kg, kc)] = ets["tseq"]

                        if kg >= cj - 2:
                            tb = kg - (cj - 2)

                            def v_m(v, kc=kc, tb=tb, j=j):
                                ix = (2 * j + tb) * 4 + kc
                                v.tensor_scalar(mskb[:], d3[:],
                                                dtb[:, ix:ix + 1],
                                                0.0, Alu.add, Alu.min)
                                return v.tensor_tensor(ps_s[kc % 2], ps_s[kc % 2],
                                                       mskb[:], Alu.add)
                            evm = p3.op("vector", v_m, incs=[("vseq", 1)],
                                        waits=[("tseq", ets["tseq"])])
                            ewait = [("vseq", evm["vseq"])]
                        else:
                            ewait = [("tseq", ets["tseq"])]

                        def s_e(s, slot=slot, kc=kc):
                            return s.activation(ptb[:, slot, :], ps_s[kc % 2],
                                                Act.Exp, bias=0.0, scale=1.0 / 32.0)
                        ese = p3.op("scalar", s_e, incs=[("sseq", 1)], waits=ewait)
                        expv[slot] = ese["sseq"]

                        def t_rs(t, slot=slot, cj=cj):
                            return t.matmul(ps_rsv[0:1, 0:512], ones[:],
                                            ptb[:, slot, :], start=(slot == 0),
                                            stop=(slot == cj * 4 - 1))
                        p3.op("tensor", t_rs, incs=[("tseq", 1)],
                              waits=[("sseq", ese["sseq"])])

                def v_rs(v):
                    return v.tensor_scalar(rse[:], ps_rsv[0:1, 0:512], 1e-30, None,
                                           Alu.add)
                ers = p3.op("vector", v_rs, incs=[("vseq", 1)],
                            waits=[("tseq", p3.now("tseq"))])

                def t_rt(t):
                    r = None
                    for qc in range(4):
                        r = t.matmul(ps_rt[:, qc:qc + 1],
                                     rse[0:1, qc * P:(qc + 1) * P], onesf[:],
                                     start=True, stop=True)
                    return r
                ert = p3.op("tensor", t_rt, incs=[("tseq", 1)],
                            waits=[("vseq", ers["vseq"])])

                ems = p3.op("vector", lambda v: v.memset(rsTin, 1.0),
                            incs=[("vseq", 1)])

                def s_rt2(s):
                    return s.copy(rsTin[:, 0:4], ps_rt[:])
                ecrt = p3.op("scalar", s_rt2, incs=[("sseq", 1)],
                             waits=[("tseq", ert["tseq"]), ("vseq", ems["vseq"])])

                def v_rt(v):
                    return v.reciprocal(rsTw, rsTin)
                ertv = p3.op("vector", v_rt, incs=[("vseq", 1)],
                             waits=[("sseq", ecrt["sseq"])])

                pv_t = {}
                for kg in range(cj):
                    svtn = "svt0" if kg % 2 == 0 else "svt1"
                    vw = []
                    if kg >= 2:
                        vw.append(("tseq", pv_t[kg - 2]))
                    elif j >= 1 and kg - 2 + prevj["cj"] >= 0:
                        vw.append(("tseq", prevj["pv_last"][kg - 2]))
                    evl = p3.op("sync",
                                lambda e, kg=kg: e.dma_start(
                                    out=vb[:, kg % 2, :, :],
                                    in_=vth[kg].rearrange("(tc p) d -> p tc d", p=P)),
                                incs=[(svtn, 16)], waits=vw)

                    wts = [(svtn, evl[svtn]), ("sseq", expv[cj * 4 - 1]),
                           ("vseq", ertv["vseq"])]

                    def t_pv(t, kg=kg, cj=cj):
                        r = None
                        for tc in range(4):
                            for qc in range(4):
                                for dvh in range(2):
                                    r = t.matmul(
                                        ps_o[qc][dvh],
                                        ptb[:, kg * 4 + tc, qc * P:(qc + 1) * P],
                                        vb[:, kg % 2, tc, dvh * 512:(dvh + 1) * 512],
                                        start=(kg == 0 and tc == 0),
                                        stop=(kg == cj - 1 and tc == 3))
                        return r
                    etpv = p3.op("tensor", t_pv, incs=[("tseq", 1)], waits=wts)
                    pv_t[kg] = etpv["tseq"]

                def v_x2(v):
                    r = None
                    for qc in range(4):
                        for dvh in range(2):
                            r = v.scalar_tensor_tensor(
                                x1j[:, qc, dvh * 512:(dvh + 1) * 512],
                                ps_o[qc][dvh], rsTw[:, qc:qc + 1],
                                x1j[:, qc, dvh * 512:(dvh + 1) * 512],
                                Alu.mult, Alu.add)
                    return r
                ex2 = p3.op("vector", v_x2, incs=[("vseq", 1)],
                            waits=[("tseq", pv_t[cj - 1]), ("sx1l", ex1l["sx1l"])])
                ex2s = p3.op("sync",
                             lambda e, j=j: e.dma_start(
                                 out=x2h[j * 512:(j + 1) * 512, :].rearrange(
                                     "(q p) d -> p q d", p=P),
                                 in_=x1j[:]),
                             incs=[("sx2", 16)], waits=[("vseq", ex2["vseq"])])
                prevj = {"q_end": q_end, "x2st": ex2s["sx2"], "x2_v": ex2["vseq"],
                         "qev_s": q_evts[-1]["sseq"], "nkt": cj, "cj": cj,
                         "smm_last": {-2: smm[(cj - 2, 3)], -1: smm[(cj - 1, 3)]},
                         "exp_last": {-2: expv[cj * 4 - 2], -1: expv[cj * 4 - 1]},
                         "pv_last": {-2: pv_t[cj - 2], -1: pv_t[cj - 1]},
                         "rsT_v": ertv["vseq"]}
            p3.ev["sync"].append((lambda e: None,
                                  [("sx2", p3.now("sx2"))], [], []))
            barrier(p3)
            with nc.Block() as blk3:
                p3.emit(blk3, semh)

        with ExitStack() as es4:
            sb4 = lambda n, s, d=F32: es4.enter_context(nc.sbuf_tensor(n, s, d))
            bufs4 = (sb4("wgt4", [P, DC, II], BF16), sb4("wut4", [P, DC, II], BF16),
                     sb4("wdt4", [P, IC, D], BF16), sb4("xb4", [P, 2, D], F32),
                     sb4("xqb4", [P, D], BF16), sb4("xqtb4", [P, DC, P], BF16),
                     sb4("hb4", [P, II], F32), sb4("hqb4", [P, II], BF16),
                     sb4("hqtb4", [P, IC, P], BF16), sb4("lnreg4", [P, 11, 176], F32),
                     sb4("stg4", [P, DC, 512], BF16))
            p4 = Prog(CNT)
            ffn_phase(p4, x2h, y_out, ("sy0", "sy1"), bufs4, False, AGW)
            p4.ev["sync"].append((lambda e: None,
                                  [("sy0", p4.now("sy0")), ("sy1", p4.now("sy1"))],
                                  [], []))
            with nc.Block() as blk4:
                p4.emit(blk4, semh)
    return nc


_NC_CACHE = None


def kernel(L0_attn_x, L0_attn_Wq, L0_attn_Wk, L0_attn_Wv,
           L0_ffn_w_gate, L0_ffn_w_up, L0_ffn_w_down):
    global _NC_CACHE
    x = np.ascontiguousarray(np.asarray(L0_attn_x, dtype=np.float32))
    wq = np.ascontiguousarray(np.asarray(L0_attn_Wq, dtype=np.float32))
    wk = np.ascontiguousarray(np.asarray(L0_attn_Wk, dtype=np.float32))
    wv = np.ascontiguousarray(np.asarray(L0_attn_Wv, dtype=np.float32))
    wg = np.ascontiguousarray(np.asarray(L0_ffn_w_gate, dtype=np.float32))
    wu = np.ascontiguousarray(np.asarray(L0_ffn_w_up, dtype=np.float32))
    wd = np.ascontiguousarray(np.asarray(L0_ffn_w_down, dtype=np.float32))
    B, T, _ = x.shape

    acat = np.zeros((NAG_A * P, D), np.float32)
    acat[0:II] = wg
    acat[IC * P:IC * P + II] = wu

    in_maps = []
    for c in range(8):
        b, par = c // 2, c % 2
        xb_seq = x[b].reshape(NKB, 512, D)
        x_own = np.concatenate([xb_seq[g] for g in ASSIGN[par]], axis=0)
        dvals = np.zeros(64, np.float32)
        for j in range(QB):
            for tb in range(2):
                kg = CJ[j] - 2 + tb
                for kc in range(4):
                    dvals[(2 * j + tb) * 4 + kc] = BIG * (
                        512.0 * (ASSIGN[par][j] - kg) - 128.0 * kc)
        dtabv = np.broadcast_to(dvals[None, :], (P, 64)).copy()
        wp = np.zeros((NGRP, P, II), np.float32)
        for i in range(6):
            wp[i, :, 0:D] = acat[(6 * c + i) * P:(6 * c + i + 1) * P]
        wp[6] = wd[c * P:(c + 1) * P]
        in_maps.append({"x_in": np.ascontiguousarray(x_own),
                        "wq": wq, "wk": wk, "wv": wv,
                        "wprep": np.ascontiguousarray(wp),
                        "dtab": np.ascontiguousarray(dtabv)})
    if _NC_CACHE is None:
        _NC_CACHE = build()
    res = run_bass_kernel_spmd(_NC_CACHE, in_maps, core_ids=list(range(8)))
    out = np.empty_like(x)
    for c in range(8):
        b, par = c // 2, c % 2
        yv = np.asarray(res.results[c]["y"]).reshape(QB, 512, D)
        for j, g in enumerate(ASSIGN[par]):
            out[b, g * 512:(g + 1) * 512, :] = yv[j]
    return out
